# version 12
# baseline (speedup 1.0000x reference)
import os
import numpy as np
import ml_dtypes
import concourse.bass as bass
import concourse.mybir as mybir
from concourse.bass_utils import run_bass_kernel_spmd

F32 = mybir.dt.float32
BF16 = mybir.dt.bfloat16
AF = mybir.ActivationFunctionType
OP = mybir.AluOpType
AX = mybir.AxisListType

NCORES = 8
S_CORE = 1024
NT = 8
DC = 32
D = 4096
NH = 8
JQ = 1024
NQ4 = 4

LN_EPS = 1e-5
DEN_EPS = 1e-6


class Plan:

    ENGINES = ("tensor", "vector", "scalar", "gpsimd", "sync")

    def __init__(self):
        self.ops = {e: [] for e in self.ENGINES}
        self.counts = {}

    def sem(self, name):
        self.counts[name] = 0
        return name

    def add(self, eng, fn, waits=(), inc=None, n=1):
        if inc is not None:
            self.counts[inc] += n
        self.ops[eng].append((fn, tuple(w for w in waits if w is not None),
                              inc, n))
        return self.counts[inc] if inc is not None else None

    def c(self, name):
        return self.counts[name]


def build_program():
    nc = bass.Bass("TRN2", target_bir_lowering=False,
                   detect_race_conditions=False)

    x_d = nc.dram_tensor("x_tm", [S_CORE, D], F32, kind="ExternalInput").ap()
    xT_d = nc.dram_tensor("xT", [D, S_CORE], F32, kind="ExternalInput").ap()
    wqT_d = nc.dram_tensor("wqT", [D, JQ], F32, kind="ExternalInput").ap()
    bq_d = nc.dram_tensor("bq", [1, JQ], F32, kind="ExternalInput").ap()
    woT_d = nc.dram_tensor("woT", [D, D], F32, kind="ExternalInput").ap()
    m_d = nc.dram_tensor("m_", [NH * 128, 128], F32, kind="ExternalInput").ap()
    zT_d = nc.dram_tensor("zT", [128, NH], F32, kind="ExternalInput").ap()
    wT_d = nc.dram_tensor("wT", [D, D], F32, kind="ExternalInput").ap()
    idb_d = nc.dram_tensor("idb", [128, 128], BF16, kind="ExternalInput").ap()
    idf_d = nc.dram_tensor("idf", [128, 128], F32, kind="ExternalInput").ap()
    out_d = nc.dram_tensor("out", [S_CORE, D], F32, kind="ExternalOutput").ap()
    retr_d = nc.dram_tensor("retr_d", [S_CORE, JQ], BF16, kind="Internal").ap()
    x2_d = nc.dram_tensor("x2_d", [S_CORE, D], F32, kind="Internal").ap()
    h2T_d = nc.dram_tensor("h2T_d", [S_CORE, D], BF16, kind="Internal").ap()
    wof_d = nc.dram_tensor("wof_d", [NH * 128, D], BF16, kind="Internal").ap()
    DBG = bool(int(os.environ.get("BASSDBG", "0")))
    if DBG:
        x2f_d = nc.dram_tensor("x2f_d", [S_CORE, D], F32,
                               kind="ExternalOutput").ap()

    base = ((nc.sbuf_base + 31) // 32) * 32
    limit = nc.sbuf_top
    cur = [base]

    def sb(name, shape, dtype, at=None):
        off = cur[0] if at is None else at
        per_part = int(np.prod(shape[1:])) * mybir.dt.size(dtype)
        assert off + per_part <= limit, f"sbuf overflow at {name}"
        h = nc.alloc_sbuf_tensor_at(name, shape, dtype, offset=off)
        if at is None:
            cur[0] = off + ((per_part + 31) // 32) * 32
        return h.ap()

    idb = sb("idb_sb", [128, 128], BF16)
    idf = sb("idf_sb", [128, 128], F32)
    msb = sb("msb", [128, NH * 128], BF16)
    ztsb = sb("ztsb", [128, NH], BF16)
    bqbc = sb("bqbc", [128, JQ], BF16)
    bqrow = sb("bqrow", [1, JQ], BF16)
    colsum = sb("colsum", [1, JQ], BF16)
    ones_r = sb("ones_r", [1, 128], BF16)
    ones_c = sb("ones_c", [128, 1], BF16)
    honq = sb("honq", [128, 128], F32)
    negmr = sb("negmr", [1, 2 * 128], BF16)
    stats = sb("stats", [128, 128], F32)
    accq = sb("accq", [128, 128], F32)
    m2r2 = sb("m2r2", [128, 2 * NT], F32)
    sqscr = sb("sqscr", [128, 1024], BF16)
    constf = sb("constf", [128, 8], F32)
    constb = sb("constb", [128, 8], BF16)
    _CONSTS = [0.0, 1.0, LN_EPS, -1.0 / 4096.0, 1.0 / 4096.0, -1.0]
    for _k, _v in enumerate(_CONSTS):
        nc.const_aps.register(F32, _v, constf[:, _k:_k + 1])
        nc.const_aps.register(BF16, _v, constb[:, _k:_k + 1])

    b_start = cur[0]
    xt = [sb(f"xt{i}", [128, D], F32) for i in range(2)]
    a_start = cur[0]
    wqt = sb("wqt", [128, DC * JQ], BF16)
    xtb = [sb(f"xtb{i}", [128, DC * 128], BF16) for i in range(4)]
    qlin = [sb(f"qlin{i}", [128, JQ], F32) for i in range(2)]
    uscr = [sb(f"uscr{i}", [128, JQ], BF16) for i in range(2)]
    ascr = [sb(f"ascr{i}", [128, JQ], BF16) for i in range(2)]
    escr = [sb(f"escr{i}", [128, JQ], BF16) for i in range(2)]
    sigqtm = [sb(f"sigqtm{i}", [128, JQ], BF16) for i in range(2)]
    sigqfm = sb("sigqfm", [128, JQ], BF16)
    retrtm = sb("retrtm", [128, JQ], BF16)
    retrfm = [sb(f"retrfm{i}", [128, JQ], BF16) for i in range(2)]
    dh_sb = sb("dh_sb", [128, NH], F32)
    rd_sb = sb("rd_sb", [128, NH], F32)
    foldg = [sb(f"foldg{i}", [128, 1024], F32) for i in range(4)]
    foldo = [sb(f"foldo{i}", [128, 1024], BF16) for i in range(2)]

    wof = sb("wof", [128, NH * D], BF16, at=a_start)
    retr_rd = [sb(f"retr_rd{i}", [128, JQ], BF16, at=a_start + 65536 + i * 2048)
               for i in range(2)]
    h2tile = [sb(f"h2tile{i}", [128, D], BF16, at=a_start + 69632 + i * 8192)
              for i in range(2)]
    h2T = [sb(f"h2T{i}", [128, D], BF16, at=a_start + 86016 + i * 8192)
           for i in range(2)]

    cb = [b_start]

    def sbB(name, shape, dtype):
        off = cb[0]
        per_part = int(np.prod(shape[1:])) * mybir.dt.size(dtype)
        h = nc.alloc_sbuf_tensor_at(name, shape, dtype, offset=off)
        cb[0] = off + ((per_part + 31) // 32) * 32
        assert cb[0] <= limit, f"sbuf overflow at {name}"
        return h.ap()

    wqh = [sbB(f"wqh{i}", [128, DC * 1024], BF16) for i in range(2)]
    ahs = [sbB(f"ahs{i}", [128, 1024], BF16) for i in range(2)]
    als = [sbB(f"als{i}", [128, 1024], BF16) for i in range(2)]
    x2q = [sbB(f"x2q{i}", [128, 1024], F32) for i in range(3)]
    h2Ts = [sbB(f"h2Ts{i}", [128, D], BF16) for i in range(2)]
    wc = [sbB(f"wc{i}", [128, 1024], F32) for i in range(3)]
    a32 = [sbB(f"a32_{i}", [128, 1024], F32) for i in range(2)]
    sgn = [sbB(f"sgn{i}", [128, 1024], BF16) for i in range(2)]
    maskb = als
    m2q = [sbB(f"m2q{i}", [128, 1024], BF16) for i in range(2)]

    T01 = nc.alloc_psum_tensor("T01", [128, 1024], F32).ap()
    T23 = nc.alloc_psum_tensor("T23", [128, 1024], F32).ap()
    T45 = nc.alloc_psum_tensor("T45", [128, 1024], F32).ap()
    T67 = nc.alloc_psum_tensor("T67", [128, 1024], F32).ap()
    T45b = T45.bitcast(BF16)
    P_tr = [T45b[:, 0:128], T45b[:, 1024:1152]]
    P_den = T45[:, 640:648]
    P_mrow = T45[0:1, 896:1024]
    P_att = [T01, T23]
    P_ffn = [T01, T23]
    P_thr = [T45, T67]
    P_cs = T67[0:1, :]

    p = Plan()
    sX = p.sem("sX")
    sXT = p.sem("sXT")
    sWQ = p.sem("sWQ")
    sWO = p.sem("sWO")
    sWOA = p.sem("sWOA")
    sK = p.sem("sK")
    sKH = p.sem("sKH")
    sWF = p.sem("sWF")
    sRT = p.sem("sRT")
    sST = p.sem("sST")
    sBL = p.sem("sBL")
    sWT = p.sem("sWT")
    sOUT = p.sem("sOUT")
    sPE = p.sem("sPE")
    sDV = p.sem("sDV")
    sSC = p.sem("sSC")
    sGP = p.sem("sGP")

    W = lambda *pairs: tuple(pairs)

    x_cnt = {}
    xt_free = [None, None]
    xtb_cnt = [None] * NT
    a1 = {}
    a2 = {}
    retr_store_cnt = [None] * NT
    x2_store_cnt = [None] * NT
    h2t_store_cnt = [None] * NT
    retrfm_st = [None, None]
    h2T_st = [None, None]
    att_free = [None, None]
    q_wc_war = [None] * 3
    q_a32_war = [None, None]
    q_sgn_war = [None, None]
    q_als_war = [None, None]
    q_thr_war = [None, None]
    b_x2q_war = [None] * 3
    b_h2t_war = [None, None]
    b_psum_war = [None, None]
    wqh_war = [None, None]
    tr_war = {"vector": None, "scalar": None}
    h2tile_war = [None, None]

    p.add("sync", lambda e: e.dma_start(out=idb, in_=idb_d), inc=sKH, n=16)
    p.add("sync", lambda e: e.dma_start(out=idf, in_=idf_d), inc=sKH, n=16)
    p.add("gpsimd", lambda e: e.dma_start(
        out=msb.rearrange("p (h e) -> p h e", h=NH),
        in_=m_d.rearrange("(h q) e -> q h e", q=128)), inc=sK, n=16)
    p.add("gpsimd", lambda e: e.dma_start(out=ztsb, in_=zT_d), inc=sK, n=16)
    p.add("gpsimd", lambda e: e.dma_start(out=bqrow, in_=bq_d), inc=sK, n=16)
    consts_done = p.c(sK)
    constsh_done = p.c(sKH)

    p.add("vector", lambda e: e.memset(ones_r, 1.0))
    p.add("vector", lambda e: e.memset(ones_c, 1.0))
    for _k, _v in enumerate(_CONSTS):
        p.add("vector", lambda e, k=_k, v=_v: e.memset(constf[:, k:k + 1], v))
        p.add("vector", lambda e, k=_k, v=_v: e.memset(constb[:, k:k + 1], v))
    memset_done = p.add("vector", lambda e: e.memset(honq, 0.00390625),
                        inc=sDV)
    p.add("scalar", lambda e: e.nop(), waits=W((sDV, memset_done)))

    def load_wqt_group(g):
        p.add("gpsimd", lambda e, g=g: e.dma_start(
            out=wqt.rearrange("p (c j) -> p c j", c=DC)[:, g * 8:(g + 1) * 8, :],
            in_=wqT_d.rearrange("(c q) j -> q c j", q=128)[:, g * 8:(g + 1) * 8, :]),
            inc=sWQ, n=16)

    def load_x(phase, ti, extra=()):
        par = ti % 2
        ws = list(extra)
        if xt_free[par] is not None:
            ws.append(xt_free[par])
        x_cnt[(phase, ti)] = p.add(
            "sync", lambda e, ti=ti, par=par: e.dma_start(
                out=xt[par], in_=x_d[ti * 128:(ti + 1) * 128, :]),
            waits=tuple(ws), inc=sX, n=16)

    def load_xtb(ti, extra=()):
        par = ti % 4
        ws = list(extra)
        xr = xT_d.rearrange("(c q) s -> q c s", q=128)
        ot = xtb[par].rearrange("p (c s) -> p c s", c=DC)
        for g in range(4):
            p.add("gpsimd", lambda e, g=g, ti=ti, ot=ot, xr=xr: e.dma_start(
                out=ot[:, g * 8:(g + 1) * 8, :],
                in_=xr[:, g * 8:(g + 1) * 8, ti * 128:(ti + 1) * 128]),
                waits=(tuple(ws) if g == 0 else ()), inc=sXT, n=16)
        xtb_cnt[ti] = p.c(sXT)

    fold_state = {"k": 0, "store": [None, None], "add": [None, None],
                  "pend": None}

    def plan_wofold_loads(h, qc, par):
        f0, f1 = par * 2, par * 2 + 1
        ws = []
        if fold_state["add"][par] is not None:
            ws.append((sGP, fold_state["add"][par]))
        c0 = p.add("sync", lambda e, h=h, qc=qc, f0=f0: e.dma_start(
            out=foldg[f0],
            in_=woT_d[(4 * h) * 128:(4 * h) * 128 + 128,
                      qc * 1024:(qc + 1) * 1024]),
            waits=tuple(ws), inc=sWO, n=16)
        c1 = p.add("sync", lambda e, h=h, qc=qc, f1=f1: e.dma_start(
            out=foldg[f1],
            in_=woT_d[(4 * h + 2) * 128:(4 * h + 2) * 128 + 128,
                      qc * 1024:(qc + 1) * 1024]),
            inc=sWO, n=16)
        p.add("gpsimd", lambda e, h=h, qc=qc, f0=f0: e.dma_start(
            out=foldg[f0],
            in_=woT_d[(4 * h + 1) * 128:(4 * h + 1) * 128 + 128,
                      qc * 1024:(qc + 1) * 1024], accum_op=OP.add),
            waits=W((sWO, c0)), inc=sWOA, n=16)
        a1 = p.add("gpsimd", lambda e, h=h, qc=qc, f1=f1: e.dma_start(
            out=foldg[f1],
            in_=woT_d[(4 * h + 3) * 128:(4 * h + 3) * 128 + 128,
                      qc * 1024:(qc + 1) * 1024], accum_op=OP.add),
            waits=W((sWO, c1)), inc=sWOA, n=16)
        return a1

    def plan_wofold_adds(h, qc, par, wo_c):
        fw = [(sWOA, wo_c)]
        if fold_state["store"][par] is not None:
            fw.append((sWF, fold_state["store"][par]))
        b = par * 2
        gp_c = p.add("gpsimd", lambda e, b=b, par=par: e.tensor_tensor(
            out=foldo[par], in0=foldg[b], in1=foldg[b + 1], op=OP.add),
            waits=tuple(fw), inc=sGP)
        fold_state["add"][par] = gp_c
        fold_state["store"][par] = p.add(
            "sync", lambda e, h=h, qc=qc, par=par: e.dma_start(
                out=wof_d[h * 128:(h + 1) * 128,
                          qc * 1024:(qc + 1) * 1024], in_=foldo[par]),
            waits=W((sGP, gp_c)), inc=sWF, n=16)

    def plan_wofold_h(h):
        for qc in range(4):
            k = fold_state["k"]
            par = k % 2
            wo_c = plan_wofold_loads(h, qc, par)
            if fold_state["pend"] is not None:
                plan_wofold_adds(*fold_state["pend"])
            fold_state["pend"] = (h, qc, par, wo_c)
            fold_state["k"] = k + 1

    def plan_prolog():
        for half in range(2):
            for c in range(DC):
                last = (c == DC - 1)
                wv = []
                if c % 8 == 0 and half == 0:
                    wv.append((sWQ, 16 * (c // 8 + 1)))
                if c == 0 and half == 0:
                    wv.append((sDV, memset_done))
                p.add("tensor", lambda e, c=c, half=half, last=last: e.matmul(
                    P_cs[:, half * 512:(half + 1) * 512], lhsT=ones_c,
                    rhs=wqt[:, c * JQ + half * 512:c * JQ + half * 512 + 512],
                    start=(c == 0), stop=last),
                    waits=tuple(wv),
                    inc=(sPE if (last and half == 1) else None))
        cs_pe = p.c(sPE)
        cs_dv = p.add("vector", lambda e: e.tensor_copy(out=colsum, in_=P_cs),
                      waits=W((sPE, cs_pe)), inc=sDV)
        for half in range(2):
            p.add("tensor", lambda e, half=half: e.matmul(
                T67[:, half * 512:(half + 1) * 512], lhsT=ones_r,
                rhs=bqrow[:, half * 512:(half + 1) * 512], start=True,
                stop=True),
                waits=(W((sK, consts_done), (sKH, constsh_done),
                         (sDV, cs_dv)) if half == 0 else ()),
                inc=(sPE if half == 1 else None))
        bq_pe = p.c(sPE)
        p.add("vector", lambda e: e.tensor_copy(out=bqbc, in_=T67),
              waits=W((sPE, bq_pe)), inc=sDV)

    def plan_stats(par, s_col, first_waits):
        a0 = s_col
        scc = None
        for q in range(4):
            p.add("scalar", lambda e, q=q, par=par, a0=a0: e.activation(
                out=sqscr, in_=xt[par][:, q * 1024:(q + 1) * 1024],
                func=AF.Copy, accum_out=accq[:, a0 + q:a0 + q + 1]),
                waits=(tuple(first_waits) if q == 0 else ()))
            scc = p.add("scalar", lambda e, q=q, par=par, a0=a0: e.activation(
                out=sqscr, in_=xt[par][:, q * 1024:(q + 1) * 1024],
                func=AF.Square, accum_out=accq[:, a0 + 4 + q:a0 + 5 + q]),
                inc=(sSC if q == 3 else None))
        s1c = p.add("vector", lambda e, s=s_col, a0=a0: e.reduce_sum(
            out=stats[:, s:s + 1], in_=accq[:, a0:a0 + 4], axis=AX.X),
            waits=W((sSC, scc)), inc=sDV)
        p.add("vector", lambda e, s=s_col, a0=a0: e.reduce_sum(
            out=stats[:, s + 1:s + 2], in_=accq[:, a0 + 4:a0 + 8], axis=AX.X),
            inc=sDV)
        mc = p.add("vector", lambda e, s=s_col: e.tensor_scalar(
            out=stats[:, s + 2:s + 3], in0=stats[:, s:s + 1],
            scalar1=1.0 / 4096.0, scalar2=None, op0=OP.mult),
            waits=W((sDV, s1c)), inc=sDV)
        nv_dv = p.add("vector", lambda e, s=s_col: e.tensor_scalar(
            out=stats[:, s + 5:s + 6], in0=stats[:, s:s + 1],
            scalar1=stats[:, s + 2:s + 3], scalar2=stats[:, s + 1:s + 2],
            op0=OP.mult, op1=OP.subtract), waits=W((sDV, mc)), inc=sDV)
        std_sc = p.add("scalar", lambda e, s=s_col: e.activation(
            out=stats[:, s + 6:s + 7], in_=stats[:, s + 5:s + 6],
            func=AF.Sqrt, scale=-1.0 / 4096.0, bias=LN_EPS),
            waits=W((sDV, nv_dv)), inc=sSC)
        r_dv = p.add("vector", lambda e, s=s_col: e.reciprocal(
            out=stats[:, s + 3:s + 4], in_=stats[:, s + 6:s + 7]),
            waits=W((sSC, std_sc)), inc=sDV)
        return scc, r_dv

    def plan_a1_part1a(ti):
        par = ti % 2
        s_col = 8 * ti
        fw = [(sX, x_cnt[("a1", ti)])]
        xread, r_dv = plan_stats(par, s_col, fw)
        xt_free[par] = (sSC, xread)
        a1[("rdv", ti)] = r_dv

    def plan_a1_part1b(ti):
        par = ti % 2
        xpar = ti % 4
        s_col = 8 * ti
        r_dv = a1[("rdv", ti)]
        q_pe = None
        for c in range(DC):
            wv = []
            if c == 0 and ti >= 1 and a1.get(("ql", ti - 1)) is not None:
                wv.append((sDV, a1[("ql", ti - 1)]))
            if c == 0:
                wv.append((sXT, xtb_cnt[ti]))
            if ti == 0 and c % 8 == 0:
                wv.append((sWQ, 16 * (c // 8 + 1)))
            for half in range(2):
                p.add("tensor", lambda e, c=c, half=half, xpar=xpar: e.matmul(
                    T01[:, half * 512:(half + 1) * 512],
                    lhsT=xtb[xpar][:, c * 128:(c + 1) * 128],
                    rhs=wqt[:, c * JQ + half * 512:c * JQ + half * 512 + 512],
                    start=(c == 0), stop=False),
                    waits=(tuple(wv) if half == 0 else ()))
        mrow_pe = p.add("tensor", lambda e, s=s_col: e.transpose(
            out=P_mrow, in_=stats[:, s:s + 1], identity=idf),
            waits=W((sDV, r_dv), (sK, consts_done), (sKH, constsh_done)),
            inc=sPE)
        negm_sc = p.add("scalar", lambda e, par=par: e.activation(
            out=negmr[0:1, par * 128:(par + 1) * 128], in_=P_mrow,
            func=AF.Copy, scale=-1.0 / 4096.0), waits=W((sPE, mrow_pe)),
            inc=sSC)
        for half in range(2):
            q_pe = p.add("tensor", lambda e, half=half, par=par: e.matmul(
                T01[:, half * 512:(half + 1) * 512],
                lhsT=negmr[0:1, par * 128:(par + 1) * 128],
                rhs=colsum[:, half * 512:(half + 1) * 512],
                start=False, stop=True),
                waits=(W((sSC, negm_sc)) if half == 0 else ()),
                inc=(sPE if half == 1 else None))
        a1[("qpe", ti)] = q_pe

        ql_dv = p.add("vector", lambda e, par=par, s=s_col:
                      e.scalar_tensor_tensor(
            out=qlin[par], in0=T01, scalar=stats[:, s + 3:s + 4], in1=bqbc,
            op0=OP.mult, op1=OP.add), waits=W((sPE, q_pe), (sDV, r_dv)),
            inc=sDV)
        a1[("ql", ti)] = ql_dv
        p.add("scalar", lambda e, par=par: e.activation(
            out=ascr[par], in_=qlin[par], func=AF.Relu),
            waits=W((sDV, ql_dv)))
        p.add("scalar", lambda e, par=par: e.activation(
            out=uscr[par], in_=qlin[par], func=AF.Relu, scale=-1.0))
        e_sc = p.add("scalar", lambda e, par=par: e.activation(
            out=escr[par], in_=uscr[par], func=AF.Exp, scale=-1.0),
            inc=sSC)
        a1[("sigq", ti)] = p.add("vector", lambda e, par=par: e.tensor_tensor(
            out=sigqtm[par], in0=ascr[par], in1=escr[par], op=OP.add),
            waits=W((sSC, e_sc)), inc=sDV)

    def plan_a1_part2(ti):
        par = ti % 2
        sigq_dv = a1[("sigq", ti)]
        for h in range(NH):
            tp = h % 2
            eng = "vector" if tp == 0 else "scalar"
            tw = [(sDV, sigq_dv)] if h == 0 else []
            if tr_war[eng] is not None:
                tw.append((sDV if eng == "vector" else sSC, tr_war[eng]))
            tpe = p.add("tensor", lambda e, h=h, tp=tp, par=par: e.transpose(
                out=P_tr[tp], in_=sigqtm[par][:, h * 128:(h + 1) * 128],
                identity=idb), waits=tuple(tw), inc=sPE)
            if eng == "vector":
                tr_war[eng] = p.add("vector", lambda e, h=h, tp=tp:
                                    e.tensor_copy(
                    out=sigqfm[:, h * 128:(h + 1) * 128], in_=P_tr[tp]),
                    waits=W((sPE, tpe)), inc=sDV)
            else:
                tr_war[eng] = p.add("scalar", lambda e, h=h, tp=tp: e.copy(
                    out=sigqfm[:, h * 128:(h + 1) * 128], in_=P_tr[tp]),
                    waits=W((sPE, tpe)), inc=sSC)
        sfm_v, sfm_s = p.c(sDV), p.c(sSC)

        nw = [(sDV, sfm_v), (sSC, sfm_s)]
        if ti == 0:
            nw.append((sK, consts_done))
        if a1.get(("retr", ti - 1)) is not None:
            nw.append((sDV, a1[("retr", ti - 1)]))
        for h in range(NH):
            p.add("tensor", lambda e, h=h: e.matmul(
                T23[:, h * 128:(h + 1) * 128],
                lhsT=sigqfm[:, h * 128:(h + 1) * 128],
                rhs=msb[:, h * 128:(h + 1) * 128], start=True, stop=True),
                waits=(tuple(nw) if h == 0 else ()))
        nd_pe = None
        for h in range(NH):
            nd_pe = p.add("tensor", lambda e, h=h: e.matmul(
                P_den[:, h:h + 1], lhsT=sigqfm[:, h * 128:(h + 1) * 128],
                rhs=ztsb[:, h:h + 1], start=True, stop=True),
                inc=(sPE if h == NH - 1 else None))

        dh_c = p.add("vector", lambda e: e.tensor_scalar(
            out=dh_sb, in0=P_den, scalar1=DEN_EPS, scalar2=None, op0=OP.add),
            waits=W((sPE, nd_pe)), inc=sDV)
        rd_c = p.add("vector", lambda e: e.reciprocal(out=rd_sb, in_=dh_sb),
                     waits=W((sDV, dh_c)), inc=sDV)
        rt_dv = None
        for h in range(NH):
            rt_dv = p.add("vector", lambda e, h=h: e.tensor_scalar(
                out=retrtm[:, h * 128:(h + 1) * 128],
                in0=T23[:, h * 128:(h + 1) * 128],
                scalar1=rd_sb[:, h:h + 1], scalar2=None, op0=OP.mult),
                waits=(W((sDV, rd_c)) if h == 0 else ()),
                inc=(sDV if h == NH - 1 else None))
        a1[("retr", ti)] = rt_dv

        for h in range(NH):
            tp = h % 2
            eng = "vector" if tp == 0 else "scalar"
            tw = [(sDV, rt_dv)] if h == 0 else []
            if tr_war[eng] is not None:
                tw.append((sDV if eng == "vector" else sSC, tr_war[eng]))
            if h < 2 and retrfm_st[par] is not None:
                tw.append((sRT, retrfm_st[par]))
            tpe = p.add("tensor", lambda e, h=h, tp=tp: e.transpose(
                out=P_tr[tp], in_=retrtm[:, h * 128:(h + 1) * 128],
                identity=idb), waits=tuple(tw), inc=sPE)
            if eng == "vector":
                tr_war[eng] = p.add("vector", lambda e, h=h, par=par, tp=tp:
                                    e.tensor_copy(
                    out=retrfm[par][:, h * 128:(h + 1) * 128], in_=P_tr[tp]),
                    waits=W((sPE, tpe)), inc=sDV)
            else:
                tr_war[eng] = p.add("scalar", lambda e, h=h, par=par, tp=tp:
                                    e.copy(
                    out=retrfm[par][:, h * 128:(h + 1) * 128], in_=P_tr[tp]),
                    waits=W((sPE, tpe)), inc=sSC)
        retrfm_st[par] = p.add("sync", lambda e, ti=ti, par=par: e.dma_start(
            out=retr_d[ti * 128:(ti + 1) * 128, :], in_=retrfm[par]),
            waits=W((sDV, p.c(sDV)), (sSC, p.c(sSC))), inc=sRT, n=16)
        retr_store_cnt[ti] = retrfm_st[par]

    wof_ready = [None]
    wof_q = []

    def plan_a2_loads(ti):
        par = ti % 2
        load_x("a2", ti)
        rw = [(sRT, retr_store_cnt[ti])]
        if ti < 2:
            rw.append((sPE, a1[("qpe", NT - 1)]))
        else:
            rw.append((sPE, a2[("attpe", ti - 2)]))
        a2[("rcnt", ti)] = p.add("sync", lambda e, ti=ti, par=par: e.dma_start(
            out=retr_rd[par], in_=retr_d[ti * 128:(ti + 1) * 128, :]),
            waits=tuple(rw), inc=sRT, n=16)

    def plan_a2_part1(ti):
        par = ti % 2
        r_cnt = a2[("rcnt", ti)]
        last_att = None
        a_pe = None
        for q4 in range(NQ4):
            pp = q4 % 2
            mw = [(sRT, r_cnt)]
            if ti < 2:
                mw.append(wof_q[q4])
            if ti == 0 and q4 < 2:
                mw.append((sDV, a1[("ql", NT - 1)] if q4 == 0
                           else a1[("retr", NT - 1)]))
            if att_free[pp] is not None:
                mw.append((sDV, att_free[pp]))
            for h in range(NH):
                for half in range(2):
                    col = q4 * 1024 + half * 512
                    a_pe = p.add("tensor",
                                 lambda e, h=h, pp=pp, col=col, half=half,
                                 par=par: e.matmul(
                        P_att[pp][:, half * 512:(half + 1) * 512],
                        lhsT=retr_rd[par][:, h * 128:(h + 1) * 128],
                        rhs=wof[:, h * D + col:h * D + col + 512],
                        start=(h == 0), stop=(h == NH - 1)),
                        waits=(tuple(mw) if (h == 0 and half == 0) else ()),
                        inc=(sPE if (h == NH - 1 and half == 1) else None))
            last_att = p.add("vector", lambda e, q4=q4, pp=pp, par=par:
                             e.tensor_tensor(
                out=xt[par][:, q4 * 1024:(q4 + 1) * 1024], in0=P_att[pp],
                in1=xt[par][:, q4 * 1024:(q4 + 1) * 1024], op=OP.add),
                waits=W((sPE, a_pe), (sX, x_cnt[("a2", ti)])), inc=sDV)
            att_free[pp] = last_att
        a2[("attpe", ti)] = a_pe
        a2[("attdv", ti)] = last_att

        x2_store_cnt[ti] = p.add("sync", lambda e, ti=ti, par=par:
                                 e.dma_start(
            out=x2_d[ti * 128:(ti + 1) * 128, :], in_=xt[par]),
            waits=W((sDV, last_att)), inc=sST, n=16)
        if DBG:
            p.add("sync", lambda e, ti=ti, par=par: e.dma_start(
                out=x2f_d[ti * 128:(ti + 1) * 128, :], in_=xt[par]),
                inc=sST, n=16)

    def plan_a2_part2a(ti):
        par = ti % 2
        s_col = 64 + 8 * ti
        fw = [(sDV, a2[("attdv", ti)])]
        xread, r_dv = plan_stats(par, s_col, fw)
        m2c = p.add("vector", lambda e, ti=ti, s=s_col: e.tensor_scalar(
            out=m2r2[:, ti:ti + 1], in0=stats[:, s:s + 1],
            scalar1=1.0 / 4096.0, scalar2=None, op0=OP.mult),
            waits=W((sDV, r_dv)), inc=sDV)
        nb_dv = p.add("vector", lambda e, ti=ti, s=s_col: e.tensor_copy(
            out=m2r2[:, NT + ti:NT + ti + 1], in_=stats[:, s + 3:s + 4]),
            waits=W((sDV, m2c)), inc=sDV)
        a2[("nb", ti)] = nb_dv

    def plan_a2_h2(ti):
        par = ti % 2
        nb_dv = a2[("nb", ti)]
        hw = [(sDV, nb_dv)]
        if h2tile_war[par] is not None:
            hw.append(h2tile_war[par])
        h2_dv = p.add("vector", lambda e, ti=ti, par=par: e.tensor_scalar(
            out=h2tile[par], in0=xt[par], scalar1=m2r2[:, ti:ti + 1],
            scalar2=m2r2[:, NT + ti:NT + ti + 1],
            op0=OP.subtract, op1=OP.mult), waits=tuple(hw), inc=sDV)
        xt_free[par] = ("both", (sDV, h2_dv), (sST, x2_store_cnt[ti]))
        a2[("h2", ti)] = h2_dv

    def plan_a2_trT(ti):
        par = ti % 2
        h2_dv = a2[("h2", ti)]
        tpe = None
        for c in range(DC):
            tp = c % 2
            eng = "vector" if tp == 0 else "scalar"
            tw = [(sDV, h2_dv)] if c == 0 else []
            if tr_war[eng] is not None:
                tw.append((sDV if eng == "vector" else sSC, tr_war[eng]))
            if c < 2 and h2T_st[par] is not None:
                tw.append((sST, h2T_st[par]))
            tpe = p.add("tensor", lambda e, c=c, tp=tp, par=par: e.transpose(
                out=P_tr[tp], in_=h2tile[par][:, c * 128:(c + 1) * 128],
                identity=idb), waits=tuple(tw), inc=sPE)
            if eng == "vector":
                tr_war[eng] = p.add("vector", lambda e, c=c, par=par, tp=tp:
                                    e.tensor_copy(
                    out=h2T[par][:, c * 128:(c + 1) * 128], in_=P_tr[tp]),
                    waits=W((sPE, tpe)), inc=sDV)
            else:
                tr_war[eng] = p.add("scalar", lambda e, c=c, par=par, tp=tp:
                                    e.copy(
                    out=h2T[par][:, c * 128:(c + 1) * 128], in_=P_tr[tp]),
                    waits=W((sPE, tpe)), inc=sSC)
        h2tile_war[par] = (sPE, tpe)
        h2T_st[par] = p.add("sync", lambda e, ti=ti, par=par: e.dma_start(
            out=h2T_d[ti * 128:(ti + 1) * 128, :], in_=h2T[par]),
            waits=W((sDV, p.c(sDV)), (sSC, p.c(sSC))), inc=sST, n=16)
        h2t_store_cnt[ti] = h2T_st[par]

    def plan_q_chunks(kq, c0, first_wait=()):
        buf = kq % 2
        last = None
        for c in range(c0, c0 + 4):
            cp = c % 2
            wcp = c % 3
            lw = list(first_wait) if c == c0 else []
            if q_wc_war[wcp] is not None:
                lw.append(q_wc_war[wcp])
            w_cnt = p.add("scalar", lambda e, c=c, wcp=wcp, kq=kq:
                          e.dma_start(
                out=wc[wcp], in_=wT_d[c * 128:(c + 1) * 128,
                                      kq * 1024:(kq + 1) * 1024]),
                waits=tuple(lw), inc=sWT, n=16)
            aw = [(sWT, w_cnt)]
            if q_a32_war[cp] is not None:
                aw.append(q_a32_war[cp])
            abs_c = p.add("scalar", lambda e, cp=cp, wcp=wcp: e.activation(
                out=a32[cp], in_=wc[wcp], func=AF.Abs), waits=tuple(aw),
                inc=sSC)
            sw = [q_sgn_war[cp]] if q_sgn_war[cp] else []
            abs_sc = p.add("scalar", lambda e, cp=cp, wcp=wcp: e.activation(
                out=sgn[cp], in_=wc[wcp], func=AF.Sign), waits=tuple(sw),
                inc=sSC)
            tw = [(sSC, abs_c)]
            if q_thr_war[cp] is not None:
                tw.append(q_thr_war[cp])
            t_pe = None
            for half in range(2):
                t_pe = p.add("tensor", lambda e, cp=cp, half=half: e.matmul(
                    P_thr[cp][:, half * 512:(half + 1) * 512], lhsT=honq,
                    rhs=a32[cp][:, half * 512:(half + 1) * 512],
                    start=True, stop=True),
                    waits=(tuple(tw) if half == 0 else ()),
                    inc=(sPE if half == 1 else None))
            mkw = [(sPE, t_pe)]
            if q_als_war[cp] is not None:
                mkw.append(q_als_war[cp])
            mk_dv = p.add("vector", lambda e, cp=cp: e.tensor_tensor(
                out=maskb[cp], in0=a32[cp], in1=P_thr[cp], op=OP.is_gt),
                waits=tuple(mkw), inc=sDV)
            q_a32_war[cp] = (sDV, mk_dv)
            ts_dv = p.add("vector", lambda e, cp=cp: e.tensor_tensor(
                out=m2q[cp], in0=sgn[cp], in1=P_thr[cp], op=OP.mult),
                inc=sDV)
            q_thr_war[cp] = (sDV, ts_dv)
            ww = []
            if wqh_war[buf] is not None and c == 0:
                ww.append(wqh_war[buf])
            wq_dv = p.add("vector", lambda e, cp=cp, buf=buf, c=c:
                          e.tensor_tensor(
                out=wqh[buf][:, c * 1024:(c + 1) * 1024], in0=maskb[cp],
                in1=m2q[cp], op=OP.mult), waits=tuple(ww), inc=sDV)
            q_wc_war[wcp] = (sSC, abs_sc)
            q_sgn_war[cp] = (sDV, ts_dv)
            q_als_war[cp] = (sDV, wq_dv)
            last = wq_dv
        return last

    b_cnt = {}

    def plan_b_loads(kq, ti):
        par = ti % 2
        xp = ti % 3
        lw = [b_x2q_war[xp]] if b_x2q_war[xp] else []
        lw.append((sST, x2_store_cnt[ti]))
        b_cnt[("x", kq, ti)] = p.add(
            "sync", lambda e, ti=ti, xp=xp, kq=kq: e.dma_start(
                out=x2q[xp], in_=x2_d[ti * 128:(ti + 1) * 128,
                                      kq * 1024:(kq + 1) * 1024]),
            waits=tuple(lw), inc=sBL, n=16)
        lw2 = [b_h2t_war[par]] if b_h2t_war[par] else []
        lw2.append((sST, h2t_store_cnt[ti]))
        b_cnt[("h", kq, ti)] = p.add(
            "sync", lambda e, ti=ti, par=par: e.dma_start(
                out=h2Ts[par], in_=h2T_d[ti * 128:(ti + 1) * 128, :]),
            waits=tuple(lw2), inc=sBL, n=16)

    def plan_b_tile(kq, ti, wqh_ready):
        buf = kq % 2
        par = ti % 2
        xp = ti % 3
        x_c = b_cnt[("x", kq, ti)]
        h_c = b_cnt[("h", kq, ti)]
        mw = [(sBL, h_c), (sDV, wqh_ready)]
        if b_psum_war[par] is not None:
            mw.append(b_psum_war[par])
        f_pe = None
        for c in range(DC):
            for half in range(2):
                f_pe = p.add("tensor",
                             lambda e, c=c, half=half, par=par, buf=buf:
                             e.matmul(
                    P_ffn[par][:, half * 512:(half + 1) * 512],
                    lhsT=h2Ts[par][:, c * 128:(c + 1) * 128],
                    rhs=wqh[buf][:, c * 1024 + half * 512:
                                 c * 1024 + half * 512 + 512],
                    start=(c == 0), stop=(c == DC - 1)),
                    waits=(tuple(mw) if (c == 0 and half == 0) else ()),
                    inc=(sPE if (c == DC - 1 and half == 1) else None))
        b_h2t_war[par] = (sPE, f_pe)
        ev_dv = p.add("vector", lambda e, par=par, xp=xp:
                      e.scalar_tensor_tensor(
            out=x2q[xp], in0=P_ffn[par], scalar=2.0, in1=x2q[xp],
            op0=OP.mult, op1=OP.add), waits=W((sPE, f_pe), (sBL, x_c)),
            inc=sDV)
        b_psum_war[par] = (sDV, ev_dv)
        b_x2q_war[xp] = (sOUT, p.add(
            "sync", lambda e, ti=ti, xp=xp, kq=kq: e.dma_start(
                out=out_d[ti * 128:(ti + 1) * 128,
                          kq * 1024:(kq + 1) * 1024], in_=x2q[xp]),
            waits=W((sDV, ev_dv)), inc=sOUT, n=16))
        return f_pe

    plan_prolog()
    load_xtb(0)
    load_wqt_group(0)
    load_xtb(1)
    load_wqt_group(1)
    load_wqt_group(2)
    load_wqt_group(3)
    load_xtb(2)
    load_xtb(3)
    load_x("a1", 0)
    load_x("a1", 1)
    plan_wofold_h(0)
    plan_wofold_h(1)

    plan_a1_part1a(0)
    plan_a1_part1a(1)
    plan_a1_part1b(0)
    load_x("a1", 2)
    plan_a1_part1b(1)
    load_x("a1", 3)
    for ti in range(NT):
        plan_a1_part2(ti)
        if ti + 2 < NT:
            plan_a1_part1a(ti + 2)
            plan_a1_part1b(ti + 2)
            if ti + 4 < NT:
                load_x("a1", ti + 4)
        if ti + 4 < NT:
            load_xtb(ti + 4, extra=((sPE, a1[("qpe", ti)]),))
        if ti + 2 < NH:
            plan_wofold_h(ti + 2)

    if fold_state["pend"] is not None:
        plan_wofold_adds(*fold_state["pend"])
        fold_state["pend"] = None
    wof_store_cnt = p.c(sWF)
    del wof_q[:]
    for g in range(4):
        p.add("sync", lambda e, g=g: e.dma_start(
            out=wof.rearrange("p (h o) -> p h o", h=NH)[:, :, g * 1024:(g + 1) * 1024],
            in_=wof_d.rearrange("(h q) o -> q h o", q=128)[:, :, g * 1024:(g + 1) * 1024]),
            waits=(W((sWF, wof_store_cnt), (sPE, a1[("qpe", NT - 1)]))
                   if g == 0 else ()), inc=sWF, n=16)
        wof_q.append((sWF, p.c(sWF)))
    wof_ready[0] = wof_q[-1]

    plan_a2_loads(0)
    plan_a2_part1(0)
    plan_a2_loads(1)
    plan_a2_part1(1)
    for ti in range(NT):
        plan_a2_part2a(ti)
        plan_a2_h2(ti)
        if ti + 2 < NT:
            plan_a2_loads(ti + 2)
        plan_a2_trT(ti)
        if ti + 2 < NT:
            plan_a2_part1(ti + 2)

    a2_done = (sST, p.c(sST))
    wqh_ready = None
    for c0 in range(0, DC, 4):
        wqh_ready = plan_q_chunks(0, c0,
                                  first_wait=((a2_done,) if c0 == 0 else ()))
    for kq in range(NQ4):
        nxt = None
        f_pe = None
        plan_b_loads(kq, 0)
        for ti in range(NT):
            if kq + 1 < NQ4:
                nxt = plan_q_chunks(kq + 1, ti * 4)
            if ti + 1 < NT:
                plan_b_loads(kq, ti + 1)
            elif kq + 1 < NQ4:
                plan_b_loads(kq + 1, 0)
            f_pe = plan_b_tile(kq, ti, wqh_ready)
        wqh_war[kq % 2] = (sPE, f_pe)
        if kq + 1 < NQ4:
            wqh_ready = nxt
    out_final = p.c(sOUT)

    sems = {name: nc.alloc_semaphore(name=name) for name in p.counts}

    def expand(w):
        if w and w[0] == "both":
            return list(w[1:])
        return [w]

    def emit(eng, ops, final_wait=None):
        for fn, waits, inc, n in ops:
            for w in waits:
                for s, v in expand(w):
                    eng.wait_ge(sems[s], v)
            ins = fn(eng)
            if inc is not None:
                ins.then_inc(sems[inc], n)
        if final_wait is not None:
            eng.wait_ge(sems[final_wait[0]], final_wait[1])

    with nc.Block() as blk:
        blk.tensor(lambda e: emit(e, p.ops["tensor"]))
        blk.vector(lambda e: emit(e, p.ops["vector"]))
        blk.scalar(lambda e: emit(e, p.ops["scalar"]))
        blk.gpsimd(lambda e: emit(e, p.ops["gpsimd"]))
        blk.sync(lambda e: emit(e, p.ops["sync"],
                                final_wait=(sOUT, out_final)))
    return nc


_NC_CACHE = [None]


def kernel(L0_attn_hidden_states, L0_attn_Wq, L0_attn_bq, L0_attn_Wk,
           L0_attn_bk, L0_attn_Wv, L0_attn_bv, L0_attn_Wo, L0_attn_M,
           L0_attn_z, L0_ffn_weight):
    _ = (L0_attn_Wk, L0_attn_bk, L0_attn_Wv, L0_attn_bv)
    x = np.asarray(L0_attn_hidden_states, np.float32).reshape(8192, 4096)
    Wq = np.asarray(L0_attn_Wq, np.float32)
    bq = np.asarray(L0_attn_bq, np.float32)
    Wo = np.asarray(L0_attn_Wo, np.float32)
    M = np.asarray(L0_attn_M, np.float32)
    z = np.asarray(L0_attn_z, np.float32)
    w = np.asarray(L0_ffn_weight, np.float32)

    xT = np.ascontiguousarray(x.T)
    wqT = np.ascontiguousarray(
        Wq.reshape(32, 128, 4096)[::4].reshape(1024, 4096).T)
    bq_sel = np.ascontiguousarray(bq.reshape(32, 128)[::4].reshape(1, 1024))
    woT = np.ascontiguousarray(Wo.T)
    m_ = np.ascontiguousarray(M.reshape(1024, 128))
    zT = np.ascontiguousarray(z.T)
    wT = np.ascontiguousarray(w.T)
    idb = np.eye(128, dtype=ml_dtypes.bfloat16)
    idf = np.eye(128, dtype=np.float32)

    if _NC_CACHE[0] is None:
        _NC_CACHE[0] = build_program()
    nc = _NC_CACHE[0]
    in_maps = []
    for c in range(NCORES):
        rows = slice(c * S_CORE, (c + 1) * S_CORE)
        in_maps.append({
            "x_tm": np.ascontiguousarray(x[rows]),
            "xT": np.ascontiguousarray(xT[:, rows]),
            "wqT": wqT, "bq": bq_sel, "woT": woT, "m_": m_, "zT": zT,
            "wT": wT, "idb": idb, "idf": idf,
        })
    res = run_bass_kernel_spmd(nc, in_maps, core_ids=list(range(NCORES)))
    out = np.concatenate([r["out"] for r in res.results], axis=0)
    return out.reshape(1, 8192, 4096)


# version 15
# speedup vs baseline: 1.0700x; 1.0700x over previous
import os
import numpy as np
import ml_dtypes
import concourse.bass as bass
import concourse.mybir as mybir
from concourse.bass_utils import run_bass_kernel_spmd

F32 = mybir.dt.float32
BF16 = mybir.dt.bfloat16
AF = mybir.ActivationFunctionType
OP = mybir.AluOpType
AX = mybir.AxisListType

NCORES = 8
S_CORE = 1024
NT = 8
DC = 32
D = 4096
NH = 8
JQ = 1024
NQ4 = 4

LN_EPS = 1e-5
DEN_EPS = 1e-6


class Plan:

    ENGINES = ("tensor", "vector", "scalar", "gpsimd", "sync")

    def __init__(self):
        self.ops = {e: [] for e in self.ENGINES}
        self.counts = {}

    def sem(self, name):
        self.counts[name] = 0
        return name

    def add(self, eng, fn, waits=(), inc=None, n=1):
        if inc is not None:
            self.counts[inc] += n
        self.ops[eng].append((fn, tuple(w for w in waits if w is not None),
                              inc, n))
        return self.counts[inc] if inc is not None else None

    def c(self, name):
        return self.counts[name]


def build_program():
    nc = bass.Bass("TRN2", target_bir_lowering=False,
                   detect_race_conditions=False)

    x_d = nc.dram_tensor("x_tm", [S_CORE, D], F32, kind="ExternalInput").ap()
    xT_d = nc.dram_tensor("xT", [D, S_CORE], F32, kind="ExternalInput").ap()
    wqT_d = nc.dram_tensor("wqT", [D, JQ], F32, kind="ExternalInput").ap()
    bq_d = nc.dram_tensor("bq", [1, JQ], F32, kind="ExternalInput").ap()
    woT_d = nc.dram_tensor("woT", [D, D], F32, kind="ExternalInput").ap()
    m_d = nc.dram_tensor("m_", [NH * 128, 128], F32, kind="ExternalInput").ap()
    zT_d = nc.dram_tensor("zT", [128, NH], F32, kind="ExternalInput").ap()
    wT_d = nc.dram_tensor("wT", [D, D], F32, kind="ExternalInput").ap()
    idb_d = nc.dram_tensor("idb", [128, 128], BF16, kind="ExternalInput").ap()
    idf_d = nc.dram_tensor("idf", [128, 128], F32, kind="ExternalInput").ap()
    out_d = nc.dram_tensor("out", [S_CORE, D], F32, kind="ExternalOutput").ap()
    retr_d = nc.dram_tensor("retr_d", [S_CORE, JQ], BF16, kind="Internal").ap()
    x2_d = nc.dram_tensor("x2_d", [S_CORE, D], F32, kind="Internal").ap()
    h2T_d = nc.dram_tensor("h2T_d", [S_CORE, D], BF16, kind="Internal").ap()
    wof_d = nc.dram_tensor("wof_d", [NH * 128, D], BF16, kind="Internal").ap()
    DBG = bool(int(os.environ.get("BASSDBG", "0")))
    if DBG:
        x2f_d = nc.dram_tensor("x2f_d", [S_CORE, D], F32,
                               kind="ExternalOutput").ap()

    base = ((nc.sbuf_base + 31) // 32) * 32
    limit = nc.sbuf_top
    cur = [base]

    def sb(name, shape, dtype, at=None):
        off = cur[0] if at is None else at
        per_part = int(np.prod(shape[1:])) * mybir.dt.size(dtype)
        assert off + per_part <= limit, f"sbuf overflow at {name}"
        h = nc.alloc_sbuf_tensor_at(name, shape, dtype, offset=off)
        if at is None:
            cur[0] = off + ((per_part + 31) // 32) * 32
        return h.ap()

    idb = sb("idb_sb", [128, 128], BF16)
    idf = sb("idf_sb", [128, 128], F32)
    msb = sb("msb", [128, NH * 128], BF16)
    ztsb = sb("ztsb", [128, NH], BF16)
    bqbc = sb("bqbc", [128, JQ], BF16)
    bqrow = sb("bqrow", [1, JQ], BF16)
    colsum = sb("colsum", [1, JQ], BF16)
    ones_r = sb("ones_r", [1, 128], BF16)
    ones_c = sb("ones_c", [128, 1], BF16)
    honq = sb("honq", [128, 128], F32)
    negmr = sb("negmr", [1, 2 * 128], BF16)
    stats = sb("stats", [128, 128], F32)
    accq = sb("accq", [128, 128], F32)
    m2r2 = sb("m2r2", [128, 2 * NT], F32)
    sqscr = sb("sqscr", [128, 1024], BF16)
    constf = sb("constf", [128, 8], F32)
    constb = sb("constb", [128, 8], BF16)
    _CONSTS = [0.0, 1.0, LN_EPS, -1.0 / 4096.0, 1.0 / 4096.0, -1.0]
    for _k, _v in enumerate(_CONSTS):
        nc.const_aps.register(F32, _v, constf[:, _k:_k + 1])
        nc.const_aps.register(BF16, _v, constb[:, _k:_k + 1])

    b_start = cur[0]
    xt = [sb(f"xt{i}", [128, D], F32) for i in range(2)]
    a_start = cur[0]
    wqt = sb("wqt", [128, DC * JQ], BF16)
    xtb = [sb(f"xtb{i}", [128, DC * 128], BF16) for i in range(4)]
    qlin = [sb(f"qlin{i}", [128, JQ], F32) for i in range(2)]
    uscr = [sb(f"uscr{i}", [128, JQ], BF16) for i in range(2)]
    ascr = [sb(f"ascr{i}", [128, JQ], BF16) for i in range(2)]
    escr = [sb(f"escr{i}", [128, JQ], BF16) for i in range(2)]
    sigqtm = [sb(f"sigqtm{i}", [128, JQ], BF16) for i in range(3)]
    sigqfm = sb("sigqfm", [128, JQ], BF16)
    retrtm = sb("retrtm", [128, JQ], BF16)
    retrfm = [sb(f"retrfm{i}", [128, JQ], BF16) for i in range(2)]
    dh_sb = sb("dh_sb", [128, NH], F32)
    rd_sb = sb("rd_sb", [128, NH], F32)
    foldg = [sb(f"foldg{i}", [128, 1024], BF16) for i in range(8)]
    foldo = [sb(f"foldo{i}", [128, 1024], BF16) for i in range(2)]

    wof = sb("wof", [128, NH * D], BF16, at=a_start)
    retr_rd = [sb(f"retr_rd{i}", [128, JQ], BF16, at=a_start + 65536 + i * 2048)
               for i in range(2)]
    h2tile = [sb(f"h2tile{i}", [128, D], BF16, at=a_start + 69632 + i * 8192)
              for i in range(2)]
    h2T = [sb(f"h2T{i}", [128, D], BF16, at=a_start + 86016 + i * 8192)
           for i in range(2)]

    cb = [b_start]

    def sbB(name, shape, dtype):
        off = cb[0]
        per_part = int(np.prod(shape[1:])) * mybir.dt.size(dtype)
        h = nc.alloc_sbuf_tensor_at(name, shape, dtype, offset=off)
        cb[0] = off + ((per_part + 31) // 32) * 32
        assert cb[0] <= limit, f"sbuf overflow at {name}"
        return h.ap()

    wqh = [sbB(f"wqh{i}", [128, DC * 1024], BF16) for i in range(2)]
    ahs = [sbB(f"ahs{i}", [128, 1024], BF16) for i in range(2)]
    als = [sbB(f"als{i}", [128, 1024], BF16) for i in range(2)]
    x2q = [sbB(f"x2q{i}", [128, 1024], F32) for i in range(3)]
    h2Ts = [sbB(f"h2Ts{i}", [128, D], BF16) for i in range(2)]
    wc = [sbB(f"wc{i}", [128, 1024], F32) for i in range(3)]
    a32 = [sbB(f"a32_{i}", [128, 1024], F32) for i in range(2)]
    sgn = [sbB(f"sgn{i}", [128, 1024], BF16) for i in range(2)]
    maskb = als
    m2q = [sbB(f"m2q{i}", [128, 1024], BF16) for i in range(2)]

    T01 = nc.alloc_psum_tensor("T01", [128, 1024], F32).ap()
    T23 = nc.alloc_psum_tensor("T23", [128, 1024], F32).ap()
    T45 = nc.alloc_psum_tensor("T45", [128, 1024], F32).ap()
    T67 = nc.alloc_psum_tensor("T67", [128, 1024], F32).ap()
    T45b = T45.bitcast(BF16)
    P_tr = [T45b[:, 0:128], T45b[:, 1024:1152]]
    P_den = T45[:, 640:648]
    P_mrow = T45[0:1, 896:1024]
    P_att = [T01, T23]
    P_ffn = [T01, T23]
    P_thr = [T45, T67]
    P_q = [T01, T67]
    P_cs = T67[0:1, :]

    p = Plan()
    sX = p.sem("sX")
    sXT = p.sem("sXT")
    sWQ = p.sem("sWQ")
    sWO = p.sem("sWO")
    sK = p.sem("sK")
    sKH = p.sem("sKH")
    sWF = p.sem("sWF")
    sRT = p.sem("sRT")
    sST = p.sem("sST")
    sBL = p.sem("sBL")
    sWT = p.sem("sWT")
    sOUT = p.sem("sOUT")
    sPE = p.sem("sPE")
    sDV = p.sem("sDV")
    sSC = p.sem("sSC")
    sGP = p.sem("sGP")

    W = lambda *pairs: tuple(pairs)

    x_cnt = {}
    xt_free = [None, None]
    xtb_cnt = [None] * NT
    a1 = {}
    a2 = {}
    retr_store_cnt = [None] * NT
    x2_store_cnt = [None] * NT
    h2t_store_cnt = [None] * NT
    retrfm_st = [None, None]
    h2T_st = [None, None]
    att_free = [None, None]
    q_wc_war = [None] * 3
    q_a32_war = [None, None]
    q_sgn_war = [None, None]
    q_als_war = [None, None]
    q_thr_war = [None, None]
    b_x2q_war = [None] * 3
    b_h2t_war = [None, None]
    b_psum_war = [None, None]
    wqh_war = [None, None]
    prolog_dv = [None]
    tr_war = {"vector": None, "scalar": None}
    h2tile_war = [None, None]

    p.add("sync", lambda e: e.dma_start(out=idb, in_=idb_d), inc=sKH, n=16)
    p.add("sync", lambda e: e.dma_start(out=idf, in_=idf_d), inc=sKH, n=16)
    p.add("gpsimd", lambda e: e.dma_start(
        out=msb.rearrange("p (h e) -> p h e", h=NH),
        in_=m_d.rearrange("(h q) e -> q h e", q=128)), inc=sK, n=16)
    p.add("gpsimd", lambda e: e.dma_start(out=ztsb, in_=zT_d), inc=sK, n=16)
    p.add("gpsimd", lambda e: e.dma_start(out=bqrow, in_=bq_d), inc=sK, n=16)
    consts_done = p.c(sK)
    constsh_done = p.c(sKH)

    p.add("vector", lambda e: e.memset(ones_r, 1.0))
    p.add("vector", lambda e: e.memset(ones_c, 1.0))
    for _k, _v in enumerate(_CONSTS):
        p.add("vector", lambda e, k=_k, v=_v: e.memset(constf[:, k:k + 1], v))
        p.add("vector", lambda e, k=_k, v=_v: e.memset(constb[:, k:k + 1], v))
    memset_done = p.add("vector", lambda e: e.memset(honq, 0.00390625),
                        inc=sDV)
    p.add("scalar", lambda e: e.nop(), waits=W((sDV, memset_done)))

    def load_wqt_group(g):
        p.add("gpsimd", lambda e, g=g: e.dma_start(
            out=wqt.rearrange("p (c j) -> p c j", c=DC)[:, g * 8:(g + 1) * 8, :],
            in_=wqT_d.rearrange("(c q) j -> q c j", q=128)[:, g * 8:(g + 1) * 8, :]),
            inc=sWQ, n=16)

    def load_x(phase, ti, extra=()):
        par = ti % 2
        ws = list(extra)
        if xt_free[par] is not None:
            ws.append(xt_free[par])
        x_cnt[(phase, ti)] = p.add(
            "sync", lambda e, ti=ti, par=par: e.dma_start(
                out=xt[par], in_=x_d[ti * 128:(ti + 1) * 128, :]),
            waits=tuple(ws), inc=sX, n=16)

    def load_xtb(ti, extra=()):
        par = ti % 4
        ws = list(extra)
        xr = xT_d.rearrange("(c q) s -> q c s", q=128)
        ot = xtb[par].rearrange("p (c s) -> p c s", c=DC)
        for g in range(4):
            p.add("gpsimd", lambda e, g=g, ti=ti, ot=ot, xr=xr: e.dma_start(
                out=ot[:, g * 8:(g + 1) * 8, :],
                in_=xr[:, g * 8:(g + 1) * 8, ti * 128:(ti + 1) * 128]),
                waits=(tuple(ws) if g == 0 else ()), inc=sXT, n=16)
        xtb_cnt[ti] = p.c(sXT)

    fold_state = {"k": 0, "store": [None, None], "pend": None}

    def plan_wofold_loads(h, qc, par):
        for g in range(4):
            row0 = (4 * h + g) * 128
            p.add("gpsimd", lambda e, row0=row0, qc=qc, g=g, par=par:
                  e.dma_start(
                out=foldg[par * 4 + g],
                in_=woT_d[row0:row0 + 128, qc * 1024:(qc + 1) * 1024]),
                inc=sWO, n=16)
        return p.c(sWO)

    def plan_wofold_adds(h, qc, par, wo_c):
        fw = [(sWO, wo_c)]
        if fold_state["store"][par] is not None:
            fw.append((sWF, fold_state["store"][par]))
        b = par * 4
        p.add("gpsimd", lambda e, b=b: e.tensor_tensor(
            out=foldg[b], in0=foldg[b], in1=foldg[b + 1], op=OP.add),
            waits=tuple(fw))
        p.add("gpsimd", lambda e, b=b: e.tensor_tensor(
            out=foldg[b + 2], in0=foldg[b + 2], in1=foldg[b + 3], op=OP.add))
        gp_c = p.add("gpsimd", lambda e, b=b, par=par: e.tensor_tensor(
            out=foldo[par], in0=foldg[b], in1=foldg[b + 2], op=OP.add),
            inc=sGP)
        fold_state["store"][par] = p.add(
            "sync", lambda e, h=h, qc=qc, par=par: e.dma_start(
                out=wof_d[h * 128:(h + 1) * 128,
                          qc * 1024:(qc + 1) * 1024], in_=foldo[par]),
            waits=W((sGP, gp_c)), inc=sWF, n=16)

    def plan_wofold_h(h):
        for qc in range(4):
            k = fold_state["k"]
            par = k % 2
            wo_c = plan_wofold_loads(h, qc, par)
            if fold_state["pend"] is not None:
                plan_wofold_adds(*fold_state["pend"])
            fold_state["pend"] = (h, qc, par, wo_c)
            fold_state["k"] = k + 1

    def plan_prolog():
        for half in range(2):
            for c in range(DC):
                last = (c == DC - 1)
                wv = []
                if c % 8 == 0 and half == 0:
                    wv.append((sWQ, 16 * (c // 8 + 1)))
                if c == 0 and half == 0:
                    wv.append((sDV, memset_done))
                p.add("tensor", lambda e, c=c, half=half, last=last: e.matmul(
                    P_cs[:, half * 512:(half + 1) * 512], lhsT=ones_c,
                    rhs=wqt[:, c * JQ + half * 512:c * JQ + half * 512 + 512],
                    start=(c == 0), stop=last),
                    waits=tuple(wv),
                    inc=(sPE if (last and half == 1) else None))
        cs_pe = p.c(sPE)
        cs_dv = p.add("vector", lambda e: e.tensor_copy(out=colsum, in_=P_cs),
                      waits=W((sPE, cs_pe)), inc=sDV)
        for half in range(2):
            p.add("tensor", lambda e, half=half: e.matmul(
                T67[:, half * 512:(half + 1) * 512], lhsT=ones_r,
                rhs=bqrow[:, half * 512:(half + 1) * 512], start=True,
                stop=True),
                waits=(W((sK, consts_done), (sKH, constsh_done),
                         (sDV, cs_dv)) if half == 0 else ()),
                inc=(sPE if half == 1 else None))
        bq_pe = p.c(sPE)
        prolog_dv[0] = p.add("vector", lambda e: e.tensor_copy(
            out=bqbc, in_=T67), waits=W((sPE, bq_pe)), inc=sDV)

    def plan_stats(par, s_col, first_waits):
        a0 = s_col
        scc = None
        for q in range(4):
            p.add("scalar", lambda e, q=q, par=par, a0=a0: e.activation(
                out=sqscr, in_=xt[par][:, q * 1024:(q + 1) * 1024],
                func=AF.Copy, accum_out=accq[:, a0 + q:a0 + q + 1]),
                waits=(tuple(first_waits) if q == 0 else ()))
            scc = p.add("scalar", lambda e, q=q, par=par, a0=a0: e.activation(
                out=sqscr, in_=xt[par][:, q * 1024:(q + 1) * 1024],
                func=AF.Square, accum_out=accq[:, a0 + 4 + q:a0 + 5 + q]),
                inc=(sSC if q == 3 else None))
        s1c = p.add("vector", lambda e, s=s_col, a0=a0: e.reduce_sum(
            out=stats[:, s:s + 1], in_=accq[:, a0:a0 + 4], axis=AX.X),
            waits=W((sSC, scc)), inc=sDV)
        p.add("vector", lambda e, s=s_col, a0=a0: e.reduce_sum(
            out=stats[:, s + 1:s + 2], in_=accq[:, a0 + 4:a0 + 8], axis=AX.X),
            inc=sDV)
        mc = p.add("vector", lambda e, s=s_col: e.tensor_scalar(
            out=stats[:, s + 2:s + 3], in0=stats[:, s:s + 1],
            scalar1=1.0 / 4096.0, scalar2=None, op0=OP.mult),
            waits=W((sDV, s1c)), inc=sDV)
        nv_dv = p.add("vector", lambda e, s=s_col: e.tensor_scalar(
            out=stats[:, s + 5:s + 6], in0=stats[:, s:s + 1],
            scalar1=stats[:, s + 2:s + 3], scalar2=stats[:, s + 1:s + 2],
            op0=OP.mult, op1=OP.subtract), waits=W((sDV, mc)), inc=sDV)
        std_sc = p.add("scalar", lambda e, s=s_col: e.activation(
            out=stats[:, s + 6:s + 7], in_=stats[:, s + 5:s + 6],
            func=AF.Sqrt, scale=-1.0 / 4096.0, bias=LN_EPS),
            waits=W((sDV, nv_dv)), inc=sSC)
        r_dv = p.add("vector", lambda e, s=s_col: e.reciprocal(
            out=stats[:, s + 3:s + 4], in_=stats[:, s + 6:s + 7]),
            waits=W((sSC, std_sc)), inc=sDV)
        return scc, r_dv

    def plan_a1_part1a(ti):
        par = ti % 2
        s_col = 8 * ti
        fw = [(sX, x_cnt[("a1", ti)])]
        xread, r_dv = plan_stats(par, s_col, fw)
        xt_free[par] = (sSC, xread)
        a1[("rdv", ti)] = r_dv

    def plan_a1_part1b(ti):
        par = ti % 2
        xpar = ti % 4
        s_col = 8 * ti
        r_dv = a1[("rdv", ti)]
        q_pe = None
        Pq = P_q[ti % 2]
        for c in range(DC):
            wv = []
            if c == 0 and ti >= 2 and a1.get(("ql", ti - 2)) is not None:
                wv.append((sDV, a1[("ql", ti - 2)]))
            if c == 0 and ti == 1:
                wv.append((sDV, prolog_dv[0]))
            if c == 0:
                wv.append((sXT, xtb_cnt[ti]))
            if ti == 0 and c % 8 == 0:
                wv.append((sWQ, 16 * (c // 8 + 1)))
            for half in range(2):
                p.add("tensor", lambda e, c=c, half=half, xpar=xpar, Pq=Pq:
                      e.matmul(
                    Pq[:, half * 512:(half + 1) * 512],
                    lhsT=xtb[xpar][:, c * 128:(c + 1) * 128],
                    rhs=wqt[:, c * JQ + half * 512:c * JQ + half * 512 + 512],
                    start=(c == 0), stop=False),
                    waits=(tuple(wv) if half == 0 else ()))
        mrow_pe = p.add("tensor", lambda e, s=s_col: e.transpose(
            out=P_mrow, in_=stats[:, s:s + 1], identity=idf),
            waits=W((sDV, r_dv), (sK, consts_done), (sKH, constsh_done)),
            inc=sPE)
        negm_sc = p.add("scalar", lambda e, par=par: e.activation(
            out=negmr[0:1, par * 128:(par + 1) * 128], in_=P_mrow,
            func=AF.Copy, scale=-1.0 / 4096.0), waits=W((sPE, mrow_pe)),
            inc=sSC)
        for half in range(2):
            q_pe = p.add("tensor", lambda e, half=half, par=par, Pq=Pq:
                         e.matmul(
                Pq[:, half * 512:(half + 1) * 512],
                lhsT=negmr[0:1, par * 128:(par + 1) * 128],
                rhs=colsum[:, half * 512:(half + 1) * 512],
                start=False, stop=True),
                waits=(W((sSC, negm_sc)) if half == 0 else ()),
                inc=(sPE if half == 1 else None))
        a1[("qpe", ti)] = q_pe

        ql_dv = p.add("vector", lambda e, par=par, s=s_col, Pq=Pq:
                      e.scalar_tensor_tensor(
            out=qlin[par], in0=Pq, scalar=stats[:, s + 3:s + 4], in1=bqbc,
            op0=OP.mult, op1=OP.add), waits=W((sPE, q_pe), (sDV, r_dv)),
            inc=sDV)
        a1[("ql", ti)] = ql_dv
        p.add("scalar", lambda e, par=par: e.activation(
            out=ascr[par], in_=qlin[par], func=AF.Relu),
            waits=W((sDV, ql_dv)))
        p.add("scalar", lambda e, par=par: e.activation(
            out=uscr[par], in_=qlin[par], func=AF.Relu, scale=-1.0))
        e_sc = p.add("scalar", lambda e, par=par: e.activation(
            out=escr[par], in_=uscr[par], func=AF.Exp, scale=-1.0),
            inc=sSC)
        a1[("sigq", ti)] = p.add("vector", lambda e, tp3=ti % 3, par=par:
                                 e.tensor_tensor(
            out=sigqtm[tp3], in0=ascr[par], in1=escr[par], op=OP.add),
            waits=W((sSC, e_sc)), inc=sDV)

    def plan_a1_part2(ti):
        par = ti % 2
        sigq_dv = a1[("sigq", ti)]
        for h in range(NH):
            tp = h % 2
            eng = "vector" if tp == 0 else "scalar"
            tw = [(sDV, sigq_dv)] if h == 0 else []
            if tr_war[eng] is not None:
                tw.append((sDV if eng == "vector" else sSC, tr_war[eng]))
            tpe = p.add("tensor", lambda e, h=h, tp=tp, tp3=ti % 3:
                        e.transpose(
                out=P_tr[tp], in_=sigqtm[tp3][:, h * 128:(h + 1) * 128],
                identity=idb), waits=tuple(tw), inc=sPE)
            if eng == "vector":
                tr_war[eng] = p.add("vector", lambda e, h=h, tp=tp:
                                    e.tensor_copy(
                    out=sigqfm[:, h * 128:(h + 1) * 128], in_=P_tr[tp]),
                    waits=W((sPE, tpe)), inc=sDV)
            else:
                tr_war[eng] = p.add("scalar", lambda e, h=h, tp=tp: e.copy(
                    out=sigqfm[:, h * 128:(h + 1) * 128], in_=P_tr[tp]),
                    waits=W((sPE, tpe)), inc=sSC)
        sfm_v, sfm_s = p.c(sDV), p.c(sSC)

        nw = [(sDV, sfm_v), (sSC, sfm_s)]
        if ti == 0:
            nw.append((sK, consts_done))
        if a1.get(("retr", ti - 1)) is not None:
            nw.append((sDV, a1[("retr", ti - 1)]))
        for h in range(NH):
            p.add("tensor", lambda e, h=h: e.matmul(
                T23[:, h * 128:(h + 1) * 128],
                lhsT=sigqfm[:, h * 128:(h + 1) * 128],
                rhs=msb[:, h * 128:(h + 1) * 128], start=True, stop=True),
                waits=(tuple(nw) if h == 0 else ()))
        nd_pe = None
        for h in range(NH):
            nd_pe = p.add("tensor", lambda e, h=h: e.matmul(
                P_den[:, h:h + 1], lhsT=sigqfm[:, h * 128:(h + 1) * 128],
                rhs=ztsb[:, h:h + 1], start=True, stop=True),
                inc=(sPE if h == NH - 1 else None))

        dh_c = p.add("vector", lambda e: e.tensor_scalar(
            out=dh_sb, in0=P_den, scalar1=DEN_EPS, scalar2=None, op0=OP.add),
            waits=W((sPE, nd_pe)), inc=sDV)
        rd_c = p.add("vector", lambda e: e.reciprocal(out=rd_sb, in_=dh_sb),
                     waits=W((sDV, dh_c)), inc=sDV)
        rt_dv = None
        for h in range(NH):
            rt_dv = p.add("vector", lambda e, h=h: e.tensor_scalar(
                out=retrtm[:, h * 128:(h + 1) * 128],
                in0=T23[:, h * 128:(h + 1) * 128],
                scalar1=rd_sb[:, h:h + 1], scalar2=None, op0=OP.mult),
                waits=(W((sDV, rd_c)) if h == 0 else ()),
                inc=(sDV if h == NH - 1 else None))
        a1[("retr", ti)] = rt_dv

        for h in range(NH):
            tp = h % 2
            eng = "vector" if tp == 0 else "scalar"
            tw = [(sDV, rt_dv)] if h == 0 else []
            if tr_war[eng] is not None:
                tw.append((sDV if eng == "vector" else sSC, tr_war[eng]))
            if h < 2 and retrfm_st[par] is not None:
                tw.append((sRT, retrfm_st[par]))
            tpe = p.add("tensor", lambda e, h=h, tp=tp: e.transpose(
                out=P_tr[tp], in_=retrtm[:, h * 128:(h + 1) * 128],
                identity=idb), waits=tuple(tw), inc=sPE)
            if eng == "vector":
                tr_war[eng] = p.add("vector", lambda e, h=h, par=par, tp=tp:
                                    e.tensor_copy(
                    out=retrfm[par][:, h * 128:(h + 1) * 128], in_=P_tr[tp]),
                    waits=W((sPE, tpe)), inc=sDV)
            else:
                tr_war[eng] = p.add("scalar", lambda e, h=h, par=par, tp=tp:
                                    e.copy(
                    out=retrfm[par][:, h * 128:(h + 1) * 128], in_=P_tr[tp]),
                    waits=W((sPE, tpe)), inc=sSC)
        retrfm_st[par] = p.add("sync", lambda e, ti=ti, par=par: e.dma_start(
            out=retr_d[ti * 128:(ti + 1) * 128, :], in_=retrfm[par]),
            waits=W((sDV, p.c(sDV)), (sSC, p.c(sSC))), inc=sRT, n=16)
        retr_store_cnt[ti] = retrfm_st[par]

    wof_ready = [None]
    wof_q = []

    def plan_a2_loads(ti):
        par = ti % 2
        load_x("a2", ti)
        rw = [(sRT, retr_store_cnt[ti])]
        if ti < 2:
            rw.append((sPE, a1[("qpe", NT - 1)]))
        else:
            rw.append((sPE, a2[("attpe", ti - 2)]))
        a2[("rcnt", ti)] = p.add("sync", lambda e, ti=ti, par=par: e.dma_start(
            out=retr_rd[par], in_=retr_d[ti * 128:(ti + 1) * 128, :]),
            waits=tuple(rw), inc=sRT, n=16)

    def plan_a2_part1(ti):
        par = ti % 2
        r_cnt = a2[("rcnt", ti)]
        last_att = None
        a_pe = None
        for q4 in range(NQ4):
            pp = q4 % 2
            mw = [(sRT, r_cnt)]
            if ti < 2:
                mw.append(wof_q[q4])
            if ti == 0 and q4 < 2:
                mw.append((sDV, a1[("ql", NT - 2)] if q4 == 0
                           else a1[("retr", NT - 1)]))
            if att_free[pp] is not None:
                mw.append((sDV, att_free[pp]))
            for h in range(NH):
                for half in range(2):
                    col = q4 * 1024 + half * 512
                    a_pe = p.add("tensor",
                                 lambda e, h=h, pp=pp, col=col, half=half,
                                 par=par: e.matmul(
                        P_att[pp][:, half * 512:(half + 1) * 512],
                        lhsT=retr_rd[par][:, h * 128:(h + 1) * 128],
                        rhs=wof[:, h * D + col:h * D + col + 512],
                        start=(h == 0), stop=(h == NH - 1)),
                        waits=(tuple(mw) if (h == 0 and half == 0) else ()),
                        inc=(sPE if (h == NH - 1 and half == 1) else None))
            last_att = p.add("vector", lambda e, q4=q4, pp=pp, par=par:
                             e.tensor_tensor(
                out=xt[par][:, q4 * 1024:(q4 + 1) * 1024], in0=P_att[pp],
                in1=xt[par][:, q4 * 1024:(q4 + 1) * 1024], op=OP.add),
                waits=W((sPE, a_pe), (sX, x_cnt[("a2", ti)])), inc=sDV)
            att_free[pp] = last_att
        a2[("attpe", ti)] = a_pe
        a2[("attdv", ti)] = last_att

        x2_store_cnt[ti] = p.add("sync", lambda e, ti=ti, par=par:
                                 e.dma_start(
            out=x2_d[ti * 128:(ti + 1) * 128, :], in_=xt[par]),
            waits=W((sDV, last_att)), inc=sST, n=16)
        if DBG:
            p.add("sync", lambda e, ti=ti, par=par: e.dma_start(
                out=x2f_d[ti * 128:(ti + 1) * 128, :], in_=xt[par]),
                inc=sST, n=16)

    def plan_a2_part2a(ti):
        par = ti % 2
        s_col = 64 + 8 * ti
        fw = [(sDV, a2[("attdv", ti)])]
        xread, r_dv = plan_stats(par, s_col, fw)
        m2c = p.add("vector", lambda e, ti=ti, s=s_col: e.tensor_scalar(
            out=m2r2[:, ti:ti + 1], in0=stats[:, s:s + 1],
            scalar1=1.0 / 4096.0, scalar2=None, op0=OP.mult),
            waits=W((sDV, r_dv)), inc=sDV)
        nb_dv = p.add("vector", lambda e, ti=ti, s=s_col: e.tensor_copy(
            out=m2r2[:, NT + ti:NT + ti + 1], in_=stats[:, s + 3:s + 4]),
            waits=W((sDV, m2c)), inc=sDV)
        a2[("nb", ti)] = nb_dv

    def plan_a2_h2(ti):
        par = ti % 2
        nb_dv = a2[("nb", ti)]
        hw = [(sDV, nb_dv)]
        if h2tile_war[par] is not None:
            hw.append(h2tile_war[par])
        h2_dv = p.add("vector", lambda e, ti=ti, par=par: e.tensor_scalar(
            out=h2tile[par], in0=xt[par], scalar1=m2r2[:, ti:ti + 1],
            scalar2=m2r2[:, NT + ti:NT + ti + 1],
            op0=OP.subtract, op1=OP.mult), waits=tuple(hw), inc=sDV)
        xt_free[par] = ("both", (sDV, h2_dv), (sST, x2_store_cnt[ti]))
        a2[("h2", ti)] = h2_dv

    def plan_a2_trT(ti):
        par = ti % 2
        h2_dv = a2[("h2", ti)]
        tpe = None
        for c in range(DC):
            tp = c % 2
            eng = "vector" if tp == 0 else "scalar"
            tw = [(sDV, h2_dv)] if c == 0 else []
            if tr_war[eng] is not None:
                tw.append((sDV if eng == "vector" else sSC, tr_war[eng]))
            if c < 2 and h2T_st[par] is not None:
                tw.append((sST, h2T_st[par]))
            tpe = p.add("tensor", lambda e, c=c, tp=tp, par=par: e.transpose(
                out=P_tr[tp], in_=h2tile[par][:, c * 128:(c + 1) * 128],
                identity=idb), waits=tuple(tw), inc=sPE)
            if eng == "vector":
                tr_war[eng] = p.add("vector", lambda e, c=c, par=par, tp=tp:
                                    e.tensor_copy(
                    out=h2T[par][:, c * 128:(c + 1) * 128], in_=P_tr[tp]),
                    waits=W((sPE, tpe)), inc=sDV)
            else:
                tr_war[eng] = p.add("scalar", lambda e, c=c, par=par, tp=tp:
                                    e.copy(
                    out=h2T[par][:, c * 128:(c + 1) * 128], in_=P_tr[tp]),
                    waits=W((sPE, tpe)), inc=sSC)
        h2tile_war[par] = (sPE, tpe)
        h2T_st[par] = p.add("sync", lambda e, ti=ti, par=par: e.dma_start(
            out=h2T_d[ti * 128:(ti + 1) * 128, :], in_=h2T[par]),
            waits=W((sDV, p.c(sDV)), (sSC, p.c(sSC))), inc=sST, n=16)
        h2t_store_cnt[ti] = h2T_st[par]

    def plan_q_chunks(kq, c0, first_wait=()):
        buf = kq % 2
        last = None
        for c in range(c0, c0 + 4):
            cp = c % 2
            wcp = c % 3
            lw = list(first_wait) if c == c0 else []
            if q_wc_war[wcp] is not None:
                lw.append(q_wc_war[wcp])
            w_cnt = p.add("scalar", lambda e, c=c, wcp=wcp, kq=kq:
                          e.dma_start(
                out=wc[wcp], in_=wT_d[c * 128:(c + 1) * 128,
                                      kq * 1024:(kq + 1) * 1024]),
                waits=tuple(lw), inc=sWT, n=16)
            aw = [(sWT, w_cnt)]
            if q_a32_war[cp] is not None:
                aw.append(q_a32_war[cp])
            abs_c = p.add("scalar", lambda e, cp=cp, wcp=wcp: e.activation(
                out=a32[cp], in_=wc[wcp], func=AF.Abs), waits=tuple(aw),
                inc=sSC)
            sw = [q_sgn_war[cp]] if q_sgn_war[cp] else []
            abs_sc = p.add("scalar", lambda e, cp=cp, wcp=wcp: e.activation(
                out=sgn[cp], in_=wc[wcp], func=AF.Sign), waits=tuple(sw),
                inc=sSC)
            tw = [(sSC, abs_c)]
            if q_thr_war[cp] is not None:
                tw.append(q_thr_war[cp])
            t_pe = None
            for half in range(2):
                t_pe = p.add("tensor", lambda e, cp=cp, half=half: e.matmul(
                    P_thr[cp][:, half * 512:(half + 1) * 512], lhsT=honq,
                    rhs=a32[cp][:, half * 512:(half + 1) * 512],
                    start=True, stop=True),
                    waits=(tuple(tw) if half == 0 else ()),
                    inc=(sPE if half == 1 else None))
            mkw = [(sPE, t_pe)]
            if q_als_war[cp] is not None:
                mkw.append(q_als_war[cp])
            mk_dv = p.add("vector", lambda e, cp=cp: e.tensor_tensor(
                out=maskb[cp], in0=a32[cp], in1=P_thr[cp], op=OP.is_gt),
                waits=tuple(mkw), inc=sDV)
            q_a32_war[cp] = (sDV, mk_dv)
            ts_dv = p.add("vector", lambda e, cp=cp: e.tensor_tensor(
                out=m2q[cp], in0=sgn[cp], in1=P_thr[cp], op=OP.mult),
                inc=sDV)
            q_thr_war[cp] = (sDV, ts_dv)
            ww = []
            if wqh_war[buf] is not None and c == 0:
                ww.append(wqh_war[buf])
            wq_dv = p.add("vector", lambda e, cp=cp, buf=buf, c=c:
                          e.tensor_tensor(
                out=wqh[buf][:, c * 1024:(c + 1) * 1024], in0=maskb[cp],
                in1=m2q[cp], op=OP.mult), waits=tuple(ww), inc=sDV)
            q_wc_war[wcp] = (sSC, abs_sc)
            q_sgn_war[cp] = (sDV, ts_dv)
            q_als_war[cp] = (sDV, wq_dv)
            last = wq_dv
        return last

    b_cnt = {}

    def plan_b_loads(kq, ti):
        par = ti % 2
        xp = ti % 3
        lw = [b_x2q_war[xp]] if b_x2q_war[xp] else []
        lw.append((sST, x2_store_cnt[ti]))
        b_cnt[("x", kq, ti)] = p.add(
            "sync", lambda e, ti=ti, xp=xp, kq=kq: e.dma_start(
                out=x2q[xp], in_=x2_d[ti * 128:(ti + 1) * 128,
                                      kq * 1024:(kq + 1) * 1024]),
            waits=tuple(lw), inc=sBL, n=16)
        lw2 = [b_h2t_war[par]] if b_h2t_war[par] else []
        lw2.append((sST, h2t_store_cnt[ti]))
        b_cnt[("h", kq, ti)] = p.add(
            "sync", lambda e, ti=ti, par=par: e.dma_start(
                out=h2Ts[par], in_=h2T_d[ti * 128:(ti + 1) * 128, :]),
            waits=tuple(lw2), inc=sBL, n=16)

    def plan_b_tile(kq, ti, wqh_ready):
        buf = kq % 2
        par = ti % 2
        xp = ti % 3
        x_c = b_cnt[("x", kq, ti)]
        h_c = b_cnt[("h", kq, ti)]
        mw = [(sBL, h_c), (sDV, wqh_ready)]
        if b_psum_war[par] is not None:
            mw.append(b_psum_war[par])
        f_pe = None
        for c in range(DC):
            for half in range(2):
                f_pe = p.add("tensor",
                             lambda e, c=c, half=half, par=par, buf=buf:
                             e.matmul(
                    P_ffn[par][:, half * 512:(half + 1) * 512],
                    lhsT=h2Ts[par][:, c * 128:(c + 1) * 128],
                    rhs=wqh[buf][:, c * 1024 + half * 512:
                                 c * 1024 + half * 512 + 512],
                    start=(c == 0), stop=(c == DC - 1)),
                    waits=(tuple(mw) if (c == 0 and half == 0) else ()),
                    inc=(sPE if (c == DC - 1 and half == 1) else None))
        b_h2t_war[par] = (sPE, f_pe)
        ev_dv = p.add("vector", lambda e, par=par, xp=xp:
                      e.scalar_tensor_tensor(
            out=x2q[xp], in0=P_ffn[par], scalar=2.0, in1=x2q[xp],
            op0=OP.mult, op1=OP.add), waits=W((sPE, f_pe), (sBL, x_c)),
            inc=sDV)
        b_psum_war[par] = (sDV, ev_dv)
        b_x2q_war[xp] = (sOUT, p.add(
            "sync", lambda e, ti=ti, xp=xp, kq=kq: e.dma_start(
                out=out_d[ti * 128:(ti + 1) * 128,
                          kq * 1024:(kq + 1) * 1024], in_=x2q[xp]),
            waits=W((sDV, ev_dv)), inc=sOUT, n=16))
        return f_pe

    plan_prolog()
    load_xtb(0)
    load_wqt_group(0)
    load_xtb(1)
    load_wqt_group(1)
    load_wqt_group(2)
    load_wqt_group(3)
    load_xtb(2)
    load_xtb(3)
    load_x("a1", 0)
    load_x("a1", 1)
    plan_wofold_h(0)
    plan_wofold_h(1)

    plan_a1_part1a(0)
    plan_a1_part1a(1)
    plan_a1_part1b(0)
    load_x("a1", 2)
    plan_a1_part1b(1)
    load_x("a1", 3)
    for ti in range(NT):
        if ti + 2 < NT:
            plan_a1_part1a(ti + 2)
            plan_a1_part1b(ti + 2)
            if ti + 4 < NT:
                load_x("a1", ti + 4)
        plan_a1_part2(ti)
        if ti + 4 < NT:
            load_xtb(ti + 4, extra=((sPE, a1[("qpe", ti)]),))
        if ti + 2 < NH:
            plan_wofold_h(ti + 2)

    if fold_state["pend"] is not None:
        plan_wofold_adds(*fold_state["pend"])
        fold_state["pend"] = None
    wof_store_cnt = p.c(sWF)
    del wof_q[:]
    for g in range(4):
        p.add("sync", lambda e, g=g: e.dma_start(
            out=wof.rearrange("p (h o) -> p h o", h=NH)[:, :, g * 1024:(g + 1) * 1024],
            in_=wof_d.rearrange("(h q) o -> q h o", q=128)[:, :, g * 1024:(g + 1) * 1024]),
            waits=(W((sWF, wof_store_cnt), (sPE, a1[("qpe", NT - 1)]))
                   if g == 0 else ()), inc=sWF, n=16)
        wof_q.append((sWF, p.c(sWF)))
    wof_ready[0] = wof_q[-1]

    plan_a2_loads(0)
    plan_a2_part1(0)
    plan_a2_loads(1)
    plan_a2_part1(1)
    for ti in range(NT):
        plan_a2_part2a(ti)
        plan_a2_h2(ti)
        if ti + 2 < NT:
            plan_a2_loads(ti + 2)
        plan_a2_trT(ti)
        if ti + 2 < NT:
            plan_a2_part1(ti + 2)

    a2_done = (sST, p.c(sST))
    wqh_ready = None
    for c0 in range(0, DC, 4):
        wqh_ready = plan_q_chunks(0, c0,
                                  first_wait=((a2_done,) if c0 == 0 else ()))
    for kq in range(NQ4):
        nxt = None
        f_pe = None
        plan_b_loads(kq, 0)
        for ti in range(NT):
            if kq + 1 < NQ4:
                nxt = plan_q_chunks(kq + 1, ti * 4)
            if ti + 1 < NT:
                plan_b_loads(kq, ti + 1)
            elif kq + 1 < NQ4:
                plan_b_loads(kq + 1, 0)
            f_pe = plan_b_tile(kq, ti, wqh_ready)
        wqh_war[kq % 2] = (sPE, f_pe)
        if kq + 1 < NQ4:
            wqh_ready = nxt
    out_final = p.c(sOUT)

    sems = {name: nc.alloc_semaphore(name=name) for name in p.counts}

    def expand(w):
        if w and w[0] == "both":
            return list(w[1:])
        return [w]

    def emit(eng, ops, final_wait=None):
        for fn, waits, inc, n in ops:
            for w in waits:
                for s, v in expand(w):
                    eng.wait_ge(sems[s], v)
            ins = fn(eng)
            if inc is not None:
                ins.then_inc(sems[inc], n)
        if final_wait is not None:
            eng.wait_ge(sems[final_wait[0]], final_wait[1])

    with nc.Block() as blk:
        blk.tensor(lambda e: emit(e, p.ops["tensor"]))
        blk.vector(lambda e: emit(e, p.ops["vector"]))
        blk.scalar(lambda e: emit(e, p.ops["scalar"]))
        blk.gpsimd(lambda e: emit(e, p.ops["gpsimd"]))
        blk.sync(lambda e: emit(e, p.ops["sync"],
                                final_wait=(sOUT, out_final)))
    return nc


_NC_CACHE = [None]


def kernel(L0_attn_hidden_states, L0_attn_Wq, L0_attn_bq, L0_attn_Wk,
           L0_attn_bk, L0_attn_Wv, L0_attn_bv, L0_attn_Wo, L0_attn_M,
           L0_attn_z, L0_ffn_weight):
    _ = (L0_attn_Wk, L0_attn_bk, L0_attn_Wv, L0_attn_bv)
    x = np.asarray(L0_attn_hidden_states, np.float32).reshape(8192, 4096)
    Wq = np.asarray(L0_attn_Wq, np.float32)
    bq = np.asarray(L0_attn_bq, np.float32)
    Wo = np.asarray(L0_attn_Wo, np.float32)
    M = np.asarray(L0_attn_M, np.float32)
    z = np.asarray(L0_attn_z, np.float32)
    w = np.asarray(L0_ffn_weight, np.float32)

    xT = np.ascontiguousarray(x.T)
    wqT = np.ascontiguousarray(
        Wq.reshape(32, 128, 4096)[::4].reshape(1024, 4096).T)
    bq_sel = np.ascontiguousarray(bq.reshape(32, 128)[::4].reshape(1, 1024))
    woT = np.ascontiguousarray(Wo.T)
    m_ = np.ascontiguousarray(M.reshape(1024, 128))
    zT = np.ascontiguousarray(z.T)
    wT = np.ascontiguousarray(w.T)
    idb = np.eye(128, dtype=ml_dtypes.bfloat16)
    idf = np.eye(128, dtype=np.float32)

    if _NC_CACHE[0] is None:
        _NC_CACHE[0] = build_program()
    nc = _NC_CACHE[0]
    in_maps = []
    for c in range(NCORES):
        rows = slice(c * S_CORE, (c + 1) * S_CORE)
        in_maps.append({
            "x_tm": np.ascontiguousarray(x[rows]),
            "xT": np.ascontiguousarray(xT[:, rows]),
            "wqT": wqT, "bq": bq_sel, "woT": woT, "m_": m_, "zT": zT,
            "wT": wT, "idb": idb, "idf": idf,
        })
    res = run_bass_kernel_spmd(nc, in_maps, core_ids=list(range(NCORES)))
    out = np.concatenate([r["out"] for r in res.results], axis=0)
    return out.reshape(1, 8192, 4096)


# version 20
# speedup vs baseline: 1.2563x; 1.1741x over previous
import os
import numpy as np
import ml_dtypes
import concourse.bass as bass
import concourse.mybir as mybir
from concourse.bass_utils import run_bass_kernel_spmd

F32 = mybir.dt.float32
BF16 = mybir.dt.bfloat16
AF = mybir.ActivationFunctionType
OP = mybir.AluOpType
AX = mybir.AxisListType

NCORES = 8
S_CORE = 1024
NT = 8
DC = 32
D = 4096
NH = 8
JQ = 1024
NQ4 = 4

LN_EPS = 1e-5
DEN_EPS = 1e-6


class Plan:

    ENGINES = ("tensor", "vector", "scalar", "gpsimd", "sync")

    def __init__(self):
        self.ops = {e: [] for e in self.ENGINES}
        self.counts = {}

    def sem(self, name):
        self.counts[name] = 0
        return name

    def add(self, eng, fn, waits=(), inc=None, n=1):
        if inc is not None:
            self.counts[inc] += n
        self.ops[eng].append((fn, tuple(w for w in waits if w is not None),
                              inc, n))
        return self.counts[inc] if inc is not None else None

    def c(self, name):
        return self.counts[name]


def build_program():
    nc = bass.Bass("TRN2", target_bir_lowering=False,
                   detect_race_conditions=False)

    x_d = nc.dram_tensor("x_tm", [S_CORE, D], F32, kind="ExternalInput").ap()
    xT_d = nc.dram_tensor("xT", [D, S_CORE], F32, kind="ExternalInput").ap()
    wqT_d = nc.dram_tensor("wqT", [D, JQ], F32, kind="ExternalInput").ap()
    bq_d = nc.dram_tensor("bq", [1, JQ], F32, kind="ExternalInput").ap()
    woT_d = nc.dram_tensor("woT", [D, D], F32, kind="ExternalInput").ap()
    m_d = nc.dram_tensor("m_", [NH * 128, 128], F32, kind="ExternalInput").ap()
    zT_d = nc.dram_tensor("zT", [128, NH], F32, kind="ExternalInput").ap()
    wT_d = nc.dram_tensor("wT", [D, D], F32, kind="ExternalInput").ap()
    idb_d = nc.dram_tensor("idb", [128, 128], BF16, kind="ExternalInput").ap()
    idf_d = nc.dram_tensor("idf", [128, 128], F32, kind="ExternalInput").ap()
    out_d = nc.dram_tensor("out", [S_CORE, D], F32, kind="ExternalOutput").ap()
    retr_d = nc.dram_tensor("retr_d", [S_CORE, JQ], BF16, kind="Internal").ap()
    x2_d = nc.dram_tensor("x2_d", [S_CORE, D], F32, kind="Internal").ap()
    h2T_d = nc.dram_tensor("h2T_d", [S_CORE, D], BF16, kind="Internal").ap()
    wof_d = nc.dram_tensor("wof_d", [NH * 128, D], BF16, kind="Internal").ap()
    DBG = bool(int(os.environ.get("BASSDBG", "0")))
    if DBG:
        x2f_d = nc.dram_tensor("x2f_d", [S_CORE, D], F32,
                               kind="ExternalOutput").ap()

    base = ((nc.sbuf_base + 31) // 32) * 32
    limit = nc.sbuf_top
    cur = [base]

    def sb(name, shape, dtype, at=None):
        off = cur[0] if at is None else at
        per_part = int(np.prod(shape[1:])) * mybir.dt.size(dtype)
        assert off + per_part <= limit, f"sbuf overflow at {name}"
        h = nc.alloc_sbuf_tensor_at(name, shape, dtype, offset=off)
        if at is None:
            cur[0] = off + ((per_part + 31) // 32) * 32
        return h.ap()

    idb = sb("idb_sb", [128, 128], BF16)
    idf = sb("idf_sb", [128, 128], F32)
    msb = sb("msb", [128, NH * 128], BF16)
    ztsb = sb("ztsb", [128, NH], BF16)
    bqbc = sb("bqbc", [128, JQ], BF16)
    bqrow = sb("bqrow", [1, JQ], BF16)
    colsum = sb("colsum", [1, JQ], BF16)
    ones_r = sb("ones_r", [1, 128], BF16)
    ones_c = sb("ones_c", [128, 1], BF16)
    honq = sb("honq", [128, 128], F32)
    negmr = sb("negmr", [1, 2 * 128], BF16)
    stats = sb("stats", [128, 128], F32)
    accq = sb("accq", [128, 128], F32)
    m2r2 = sb("m2r2", [128, 2 * NT], F32)
    sqscr = sb("sqscr", [128, 1024], BF16)
    constf = sb("constf", [128, 8], F32)
    constb = sb("constb", [128, 8], BF16)
    _CONSTS = [0.0, 1.0, LN_EPS, -1.0 / 4096.0, 1.0 / 4096.0, -1.0]
    for _k, _v in enumerate(_CONSTS):
        nc.const_aps.register(F32, _v, constf[:, _k:_k + 1])
        nc.const_aps.register(BF16, _v, constb[:, _k:_k + 1])

    b_start = cur[0]
    xt = [sb(f"xt{i}", [128, D], F32) for i in range(2)]
    a_start = cur[0]
    wqt = sb("wqt", [128, DC * JQ], BF16)
    xtb = [sb(f"xtb{i}", [128, DC * 128], BF16) for i in range(4)]
    qlin = [sb(f"qlin{i}", [128, JQ], F32) for i in range(2)]
    uscr = [sb(f"uscr{i}", [128, JQ], BF16) for i in range(2)]
    ascr = [sb(f"ascr{i}", [128, JQ], BF16) for i in range(2)]
    escr = [sb(f"escr{i}", [128, JQ], BF16) for i in range(2)]
    sigqtm = [sb(f"sigqtm{i}", [128, JQ], BF16) for i in range(2)]
    sigqfm = sb("sigqfm", [128, JQ], BF16)
    retrtm = sb("retrtm", [128, JQ], BF16)
    retrfm = [sb(f"retrfm{i}", [128, JQ], BF16) for i in range(2)]
    dh_sb = sb("dh_sb", [128, NH], F32)
    rd_sb = sb("rd_sb", [128, NH], F32)
    foldg = [sb(f"foldg{i}", [128, 1024], BF16) for i in range(8)]
    foldo = [sb(f"foldo{i}", [128, 1024], BF16) for i in range(2)]

    wof = sb("wof", [128, NH * D], BF16, at=a_start)
    retr_rd = [sb(f"retr_rd{i}", [128, JQ], BF16, at=a_start + 65536 + i * 2048)
               for i in range(2)]
    h2tile = [sb(f"h2tile{i}", [128, D], BF16, at=a_start + 69632 + i * 8192)
              for i in range(2)]
    h2T = [sb(f"h2T{i}", [128, D], BF16, at=a_start + 86016 + i * 8192)
           for i in range(2)]

    cb = [b_start]

    def sbB(name, shape, dtype):
        off = cb[0]
        per_part = int(np.prod(shape[1:])) * mybir.dt.size(dtype)
        h = nc.alloc_sbuf_tensor_at(name, shape, dtype, offset=off)
        cb[0] = off + ((per_part + 31) // 32) * 32
        assert cb[0] <= limit, f"sbuf overflow at {name}"
        return h.ap()

    wqh = [sbB(f"wqh{i}", [128, DC * 1024], BF16) for i in range(2)]
    als = [sbB(f"als{i}", [128, 1024], BF16) for i in range(2)]
    x2q = [sbB(f"x2q{i}", [128, 1024], F32) for i in range(3)]
    h2Ts = [sbB(f"h2Ts{i}", [128, D], BF16) for i in range(2)]
    wc = [sbB(f"wc{i}", [128, 1024], F32) for i in range(4)]
    a32 = [sbB(f"a32_{i}", [128, 1024], F32) for i in range(2)]
    sgn = [sbB(f"sgn{i}", [128, 1024], BF16) for i in range(2)]
    maskb = als
    m2q = [sbB(f"m2q{i}", [128, 1024], BF16) for i in range(2)]

    T01 = nc.alloc_psum_tensor("T01", [128, 1024], F32).ap()
    T23 = nc.alloc_psum_tensor("T23", [128, 1024], F32).ap()
    T45 = nc.alloc_psum_tensor("T45", [128, 1024], F32).ap()
    T67 = nc.alloc_psum_tensor("T67", [128, 1024], F32).ap()
    T45b = T45.bitcast(BF16)
    T67b = T67.bitcast(BF16)
    P_tr = [T45b[:, 0:128], T45b[:, 1024:1152],
            T67b[:, 0:128], T67b[:, 1024:1152]]
    P_den = T45[:, 640:648]
    P_mrow = T45[0:1, 896:1024]
    P_att = [T01, T23]
    P_ffn = [T01, T23]
    P_thr = [T45, T67]
    P_cs = T67[0:1, :]

    p = Plan()
    sX = p.sem("sX")
    sXT = p.sem("sXT")
    sWQ = p.sem("sWQ")
    sWO = p.sem("sWO")
    sK = p.sem("sK")
    sKH = p.sem("sKH")
    sWF = p.sem("sWF")
    sRT = p.sem("sRT")
    sST = p.sem("sST")
    sBL = p.sem("sBL")
    sWT = p.sem("sWT")
    sOUT = p.sem("sOUT")
    sPE = p.sem("sPE")
    sDV = p.sem("sDV")
    sSC = p.sem("sSC")
    sGP = p.sem("sGP")

    W = lambda *pairs: tuple(pairs)

    x_cnt = {}
    xt_free = [None, None]
    xtb_cnt = [None] * NT
    a1 = {}
    a2 = {}
    retr_store_cnt = [None] * NT
    x2_store_cnt = [None] * NT
    h2t_store_cnt = [None] * NT
    retrfm_st = [None, None]
    h2T_st = [None, None]
    att_free = [None, None]
    q_wc_war = [None] * 4
    q_a32_war = [None, None]
    q_sgn_war = [None, None]
    q_als_war = [None, None]
    q_thr_war = [None, None]
    b_x2q_war = [None] * 3
    b_h2t_war = [None, None]
    b_psum_war = [None, None]
    wqh_war = [None, None]
    prolog_dv = [None]
    tr_war = [None, None, None, None]
    h2tile_war = [None, None]

    p.add("sync", lambda e: e.dma_start(out=idb, in_=idb_d), inc=sKH, n=16)
    p.add("sync", lambda e: e.dma_start(out=idf, in_=idf_d), inc=sKH, n=16)
    p.add("gpsimd", lambda e: e.dma_start(
        out=msb.rearrange("p (h e) -> p h e", h=NH),
        in_=m_d.rearrange("(h q) e -> q h e", q=128)), inc=sK, n=16)
    p.add("gpsimd", lambda e: e.dma_start(out=ztsb, in_=zT_d), inc=sK, n=16)
    p.add("gpsimd", lambda e: e.dma_start(out=bqrow, in_=bq_d), inc=sK, n=16)
    consts_done = p.c(sK)
    constsh_done = p.c(sKH)

    p.add("vector", lambda e: e.memset(ones_r, 1.0))
    p.add("vector", lambda e: e.memset(ones_c, 1.0))
    for _k, _v in enumerate(_CONSTS):
        p.add("vector", lambda e, k=_k, v=_v: e.memset(constf[:, k:k + 1], v))
        p.add("vector", lambda e, k=_k, v=_v: e.memset(constb[:, k:k + 1], v))
    memset_done = p.add("vector", lambda e: e.memset(honq, 0.00390625),
                        inc=sDV)
    p.add("scalar", lambda e: e.nop(), waits=W((sDV, memset_done)))

    def load_wqt_group(g):
        p.add("gpsimd", lambda e, g=g: e.dma_start(
            out=wqt.rearrange("p (c j) -> p c j", c=DC)[:, g * 8:(g + 1) * 8, :],
            in_=wqT_d.rearrange("(c q) j -> q c j", q=128)[:, g * 8:(g + 1) * 8, :]),
            inc=sWQ, n=16)

    def load_x(phase, ti, extra=()):
        par = ti % 2
        ws = list(extra)
        if xt_free[par] is not None:
            ws.append(xt_free[par])
        x_cnt[(phase, ti)] = p.add(
            "sync", lambda e, ti=ti, par=par: e.dma_start(
                out=xt[par], in_=x_d[ti * 128:(ti + 1) * 128, :]),
            waits=tuple(ws), inc=sX, n=16)

    def load_xtb(ti, extra=()):
        par = ti % 4
        ws = list(extra)
        xr = xT_d.rearrange("(c q) s -> q c s", q=128)
        ot = xtb[par].rearrange("p (c s) -> p c s", c=DC)
        for g in range(4):
            p.add("gpsimd", lambda e, g=g, ti=ti, ot=ot, xr=xr: e.dma_start(
                out=ot[:, g * 8:(g + 1) * 8, :],
                in_=xr[:, g * 8:(g + 1) * 8, ti * 128:(ti + 1) * 128]),
                waits=(tuple(ws) if g == 0 else ()), inc=sXT, n=16)
        xtb_cnt[ti] = p.c(sXT)

    fold_state = {"k": 0, "store": [None, None], "pend": None}

    def plan_wofold_loads(h, qc, par):
        for g in range(4):
            row0 = (4 * h + g) * 128
            p.add("gpsimd", lambda e, row0=row0, qc=qc, g=g, par=par:
                  e.dma_start(
                out=foldg[par * 4 + g],
                in_=woT_d[row0:row0 + 128, qc * 1024:(qc + 1) * 1024]),
                inc=sWO, n=16)
        return p.c(sWO)

    def plan_wofold_adds(h, qc, par, wo_c):
        fw = [(sWO, wo_c)]
        if fold_state["store"][par] is not None:
            fw.append((sWF, fold_state["store"][par]))
        b = par * 4
        p.add("gpsimd", lambda e, b=b: e.tensor_tensor(
            out=foldg[b], in0=foldg[b], in1=foldg[b + 1], op=OP.add),
            waits=tuple(fw))
        p.add("gpsimd", lambda e, b=b: e.tensor_tensor(
            out=foldg[b + 2], in0=foldg[b + 2], in1=foldg[b + 3], op=OP.add))
        gp_c = p.add("gpsimd", lambda e, b=b, par=par: e.tensor_tensor(
            out=foldo[par], in0=foldg[b], in1=foldg[b + 2], op=OP.add),
            inc=sGP)
        fold_state["store"][par] = p.add(
            "sync", lambda e, h=h, qc=qc, par=par: e.dma_start(
                out=wof_d[h * 128:(h + 1) * 128,
                          qc * 1024:(qc + 1) * 1024], in_=foldo[par]),
            waits=W((sGP, gp_c)), inc=sWF, n=16)

    def plan_wofold_h(h):
        for qc in range(4):
            k = fold_state["k"]
            par = k % 2
            wo_c = plan_wofold_loads(h, qc, par)
            if fold_state["pend"] is not None:
                plan_wofold_adds(*fold_state["pend"])
            fold_state["pend"] = (h, qc, par, wo_c)
            fold_state["k"] = k + 1

    def plan_prolog():
        for half in range(2):
            for c in range(DC):
                last = (c == DC - 1)
                wv = []
                if c % 8 == 0 and half == 0:
                    wv.append((sWQ, 16 * (c // 8 + 1)))
                if c == 0 and half == 0:
                    wv.append((sDV, memset_done))
                p.add("tensor", lambda e, c=c, half=half, last=last: e.matmul(
                    P_cs[:, half * 512:(half + 1) * 512], lhsT=ones_c,
                    rhs=wqt[:, c * JQ + half * 512:c * JQ + half * 512 + 512],
                    start=(c == 0), stop=last),
                    waits=tuple(wv),
                    inc=(sPE if (last and half == 1) else None))
        cs_pe = p.c(sPE)
        cs_dv = p.add("vector", lambda e: e.tensor_copy(out=colsum, in_=P_cs),
                      waits=W((sPE, cs_pe)), inc=sDV)
        for half in range(2):
            p.add("tensor", lambda e, half=half: e.matmul(
                T67[:, half * 512:(half + 1) * 512], lhsT=ones_r,
                rhs=bqrow[:, half * 512:(half + 1) * 512], start=True,
                stop=True),
                waits=(W((sK, consts_done), (sKH, constsh_done),
                         (sDV, cs_dv)) if half == 0 else ()),
                inc=(sPE if half == 1 else None))
        bq_pe = p.c(sPE)
        prolog_dv[0] = p.add("vector", lambda e: e.tensor_copy(
            out=bqbc, in_=T67), waits=W((sPE, bq_pe)), inc=sDV)

    def plan_stats(par, s_col, first_waits):
        a0 = s_col
        scc = None
        for q in range(4):
            p.add("scalar", lambda e, q=q, par=par, a0=a0: e.activation(
                out=sqscr, in_=xt[par][:, q * 1024:(q + 1) * 1024],
                func=AF.Copy, accum_out=accq[:, a0 + q:a0 + q + 1]),
                waits=(tuple(first_waits) if q == 0 else ()))
            scc = p.add("scalar", lambda e, q=q, par=par, a0=a0: e.activation(
                out=sqscr, in_=xt[par][:, q * 1024:(q + 1) * 1024],
                func=AF.Square, accum_out=accq[:, a0 + 4 + q:a0 + 5 + q]),
                inc=(sSC if q == 3 else None))
        s1c = p.add("vector", lambda e, s=s_col, a0=a0: e.reduce_sum(
            out=stats[:, s:s + 1], in_=accq[:, a0:a0 + 4], axis=AX.X),
            waits=W((sSC, scc)), inc=sDV)
        p.add("vector", lambda e, s=s_col, a0=a0: e.reduce_sum(
            out=stats[:, s + 1:s + 2], in_=accq[:, a0 + 4:a0 + 8], axis=AX.X),
            inc=sDV)
        mc = p.add("vector", lambda e, s=s_col: e.tensor_scalar(
            out=stats[:, s + 2:s + 3], in0=stats[:, s:s + 1],
            scalar1=1.0 / 4096.0, scalar2=None, op0=OP.mult),
            waits=W((sDV, s1c)), inc=sDV)
        nv_dv = p.add("vector", lambda e, s=s_col: e.tensor_scalar(
            out=stats[:, s + 5:s + 6], in0=stats[:, s:s + 1],
            scalar1=stats[:, s + 2:s + 3], scalar2=stats[:, s + 1:s + 2],
            op0=OP.mult, op1=OP.subtract), waits=W((sDV, mc)), inc=sDV)
        std_sc = p.add("scalar", lambda e, s=s_col: e.activation(
            out=stats[:, s + 6:s + 7], in_=stats[:, s + 5:s + 6],
            func=AF.Sqrt, scale=-1.0 / 4096.0, bias=LN_EPS),
            waits=W((sDV, nv_dv)), inc=sSC)
        r_dv = p.add("vector", lambda e, s=s_col: e.reciprocal(
            out=stats[:, s + 3:s + 4], in_=stats[:, s + 6:s + 7]),
            waits=W((sSC, std_sc)), inc=sDV)
        return scc, r_dv

    def plan_a1_part1a(ti):
        par = ti % 2
        s_col = 8 * ti
        fw = [(sX, x_cnt[("a1", ti)])]
        xread, r_dv = plan_stats(par, s_col, fw)
        xt_free[par] = (sSC, xread)
        a1[("rdv", ti)] = r_dv

    def plan_a1_part1b(ti):
        par = ti % 2
        xpar = ti % 4
        s_col = 8 * ti
        r_dv = a1[("rdv", ti)]
        q_pe = None
        for c in range(DC):
            wv = []
            if c == 0 and ti >= 1 and a1.get(("ql", ti - 1)) is not None:
                wv.append((sDV, a1[("ql", ti - 1)]))
            if c == 0:
                wv.append((sXT, xtb_cnt[ti]))
            if ti == 0 and c % 8 == 0:
                wv.append((sWQ, 16 * (c // 8 + 1)))
            for half in range(2):
                p.add("tensor", lambda e, c=c, half=half, xpar=xpar: e.matmul(
                    T01[:, half * 512:(half + 1) * 512],
                    lhsT=xtb[xpar][:, c * 128:(c + 1) * 128],
                    rhs=wqt[:, c * JQ + half * 512:c * JQ + half * 512 + 512],
                    start=(c == 0), stop=False),
                    waits=(tuple(wv) if half == 0 else ()))
        mrow_pe = p.add("tensor", lambda e, s=s_col: e.transpose(
            out=P_mrow, in_=stats[:, s:s + 1], identity=idf),
            waits=W((sDV, r_dv), (sK, consts_done), (sKH, constsh_done)),
            inc=sPE)
        negm_sc = p.add("scalar", lambda e, par=par: e.activation(
            out=negmr[0:1, par * 128:(par + 1) * 128], in_=P_mrow,
            func=AF.Copy, scale=-1.0 / 4096.0), waits=W((sPE, mrow_pe)),
            inc=sSC)
        for half in range(2):
            q_pe = p.add("tensor", lambda e, half=half, par=par: e.matmul(
                T01[:, half * 512:(half + 1) * 512],
                lhsT=negmr[0:1, par * 128:(par + 1) * 128],
                rhs=colsum[:, half * 512:(half + 1) * 512],
                start=False, stop=True),
                waits=(W((sSC, negm_sc)) if half == 0 else ()),
                inc=(sPE if half == 1 else None))
        a1[("qpe", ti)] = q_pe

        ql_dv = p.add("vector", lambda e, par=par, s=s_col:
                      e.scalar_tensor_tensor(
            out=qlin[par], in0=T01, scalar=stats[:, s + 3:s + 4], in1=bqbc,
            op0=OP.mult, op1=OP.add), waits=W((sPE, q_pe), (sDV, r_dv)),
            inc=sDV)
        a1[("ql", ti)] = ql_dv
        p.add("scalar", lambda e, par=par: e.activation(
            out=ascr[par], in_=qlin[par], func=AF.Relu),
            waits=W((sDV, ql_dv)))
        p.add("scalar", lambda e, par=par: e.activation(
            out=uscr[par], in_=qlin[par], func=AF.Relu, scale=-1.0))
        e_sc = p.add("scalar", lambda e, par=par: e.activation(
            out=escr[par], in_=uscr[par], func=AF.Exp, scale=-1.0),
            inc=sSC)
        a1[("sigq", ti)] = p.add("vector", lambda e, par=par: e.tensor_tensor(
            out=sigqtm[par], in0=ascr[par], in1=escr[par], op=OP.add),
            waits=W((sSC, e_sc)), inc=sDV)

    def plan_a1_part2(ti):
        par = ti % 2
        sigq_dv = a1[("sigq", ti)]
        for h in range(NH):
            tp = h % 4
            eng = "vector" if tp % 2 == 0 else "scalar"
            tw = [(sDV, sigq_dv)] if h == 0 else []
            if tr_war[tp] is not None:
                tw.append((sDV if eng == "vector" else sSC, tr_war[tp]))
            if ti == 0 and h in (2, 3):
                tw.append((sDV, prolog_dv[0]))
            tpe = p.add("tensor", lambda e, h=h, tp=tp, par=par: e.transpose(
                out=P_tr[tp], in_=sigqtm[par][:, h * 128:(h + 1) * 128],
                identity=idb), waits=tuple(tw), inc=sPE)
            if eng == "vector":
                tr_war[tp] = p.add("vector", lambda e, h=h, tp=tp:
                                   e.tensor_copy(
                    out=sigqfm[:, h * 128:(h + 1) * 128], in_=P_tr[tp]),
                    waits=W((sPE, tpe)), inc=sDV)
            else:
                tr_war[tp] = p.add("scalar", lambda e, h=h, tp=tp: e.copy(
                    out=sigqfm[:, h * 128:(h + 1) * 128], in_=P_tr[tp]),
                    waits=W((sPE, tpe)), inc=sSC)
        sfm_v, sfm_s = p.c(sDV), p.c(sSC)

        nw = [(sDV, sfm_v), (sSC, sfm_s)]
        if ti == 0:
            nw.append((sK, consts_done))
        if a1.get(("retr", ti - 1)) is not None:
            nw.append((sDV, a1[("retr", ti - 1)]))
        for h in range(NH):
            p.add("tensor", lambda e, h=h: e.matmul(
                T23[:, h * 128:(h + 1) * 128],
                lhsT=sigqfm[:, h * 128:(h + 1) * 128],
                rhs=msb[:, h * 128:(h + 1) * 128], start=True, stop=True),
                waits=(tuple(nw) if h == 0 else ()))
        nd_pe = None
        for h in range(NH):
            nd_pe = p.add("tensor", lambda e, h=h: e.matmul(
                P_den[:, h:h + 1], lhsT=sigqfm[:, h * 128:(h + 1) * 128],
                rhs=ztsb[:, h:h + 1], start=True, stop=True),
                inc=(sPE if h == NH - 1 else None))

        dh_c = p.add("vector", lambda e: e.tensor_scalar(
            out=dh_sb, in0=P_den, scalar1=DEN_EPS, scalar2=None, op0=OP.add),
            waits=W((sPE, nd_pe)), inc=sDV)
        rd_c = p.add("vector", lambda e: e.reciprocal(out=rd_sb, in_=dh_sb),
                     waits=W((sDV, dh_c)), inc=sDV)
        rt_dv = None
        for h in range(NH):
            rt_dv = p.add("vector", lambda e, h=h: e.tensor_scalar(
                out=retrtm[:, h * 128:(h + 1) * 128],
                in0=T23[:, h * 128:(h + 1) * 128],
                scalar1=rd_sb[:, h:h + 1], scalar2=None, op0=OP.mult),
                waits=(W((sDV, rd_c)) if h == 0 else ()),
                inc=(sDV if h == NH - 1 else None))
        a1[("retr", ti)] = rt_dv

        for h in range(NH):
            tp = h % 4
            eng = "vector" if tp % 2 == 0 else "scalar"
            tw = [(sDV, rt_dv)] if h == 0 else []
            if tr_war[tp] is not None:
                tw.append((sDV if eng == "vector" else sSC, tr_war[tp]))
            if h < 4 and retrfm_st[par] is not None:
                tw.append((sRT, retrfm_st[par]))
            tpe = p.add("tensor", lambda e, h=h, tp=tp: e.transpose(
                out=P_tr[tp], in_=retrtm[:, h * 128:(h + 1) * 128],
                identity=idb), waits=tuple(tw), inc=sPE)
            if eng == "vector":
                tr_war[tp] = p.add("vector", lambda e, h=h, par=par, tp=tp:
                                   e.tensor_copy(
                    out=retrfm[par][:, h * 128:(h + 1) * 128], in_=P_tr[tp]),
                    waits=W((sPE, tpe)), inc=sDV)
            else:
                tr_war[tp] = p.add("scalar", lambda e, h=h, par=par, tp=tp:
                                   e.copy(
                    out=retrfm[par][:, h * 128:(h + 1) * 128], in_=P_tr[tp]),
                    waits=W((sPE, tpe)), inc=sSC)
        retrfm_st[par] = p.add("sync", lambda e, ti=ti, par=par: e.dma_start(
            out=retr_d[ti * 128:(ti + 1) * 128, :], in_=retrfm[par]),
            waits=W((sDV, p.c(sDV)), (sSC, p.c(sSC))), inc=sRT, n=16)
        retr_store_cnt[ti] = retrfm_st[par]

    wof_ready = [None]
    wof_q = []

    def plan_a2_loads(ti):
        par = ti % 2
        load_x("a2", ti)
        rw = [(sRT, retr_store_cnt[ti])]
        if ti < 2:
            rw.append((sPE, a1[("qpe", NT - 1)]))
        else:
            rw.append((sPE, a2[("attpe", ti - 2)]))
        a2[("rcnt", ti)] = p.add("sync", lambda e, ti=ti, par=par: e.dma_start(
            out=retr_rd[par], in_=retr_d[ti * 128:(ti + 1) * 128, :]),
            waits=tuple(rw), inc=sRT, n=16)

    def plan_a2_part1(ti):
        par = ti % 2
        r_cnt = a2[("rcnt", ti)]
        last_att = None
        a_pe = None
        for q4 in range(NQ4):
            pp = q4 % 2
            mw = [(sRT, r_cnt)]
            if ti < 2:
                mw.append(wof_q[q4])
            if ti == 0 and q4 < 2:
                mw.append((sDV, a1[("ql", NT - 1)] if q4 == 0
                           else a1[("retr", NT - 1)]))
            if att_free[pp] is not None:
                mw.append((sDV, att_free[pp]))
            for h in range(NH):
                for half in range(2):
                    col = q4 * 1024 + half * 512
                    a_pe = p.add("tensor",
                                 lambda e, h=h, pp=pp, col=col, half=half,
                                 par=par: e.matmul(
                        P_att[pp][:, half * 512:(half + 1) * 512],
                        lhsT=retr_rd[par][:, h * 128:(h + 1) * 128],
                        rhs=wof[:, h * D + col:h * D + col + 512],
                        start=(h == 0), stop=(h == NH - 1)),
                        waits=(tuple(mw) if (h == 0 and half == 0) else ()),
                        inc=(sPE if (h == NH - 1 and half == 1) else None))
            last_att = p.add("vector", lambda e, q4=q4, pp=pp, par=par:
                             e.tensor_tensor(
                out=xt[par][:, q4 * 1024:(q4 + 1) * 1024], in0=P_att[pp],
                in1=xt[par][:, q4 * 1024:(q4 + 1) * 1024], op=OP.add),
                waits=W((sPE, a_pe), (sX, x_cnt[("a2", ti)])), inc=sDV)
            att_free[pp] = last_att
        a2[("attpe", ti)] = a_pe
        a2[("attdv", ti)] = last_att

        x2_store_cnt[ti] = p.add("sync", lambda e, ti=ti, par=par:
                                 e.dma_start(
            out=x2_d[ti * 128:(ti + 1) * 128, :], in_=xt[par]),
            waits=W((sDV, last_att)), inc=sST, n=16)
        if DBG:
            p.add("sync", lambda e, ti=ti, par=par: e.dma_start(
                out=x2f_d[ti * 128:(ti + 1) * 128, :], in_=xt[par]),
                inc=sST, n=16)

    def plan_a2_part2a(ti):
        par = ti % 2
        s_col = 64 + 8 * ti
        fw = [(sDV, a2[("attdv", ti)])]
        xread, r_dv = plan_stats(par, s_col, fw)
        m2c = p.add("vector", lambda e, ti=ti, s=s_col: e.tensor_scalar(
            out=m2r2[:, ti:ti + 1], in0=stats[:, s:s + 1],
            scalar1=1.0 / 4096.0, scalar2=None, op0=OP.mult),
            waits=W((sDV, r_dv)), inc=sDV)
        nb_dv = p.add("vector", lambda e, ti=ti, s=s_col: e.tensor_copy(
            out=m2r2[:, NT + ti:NT + ti + 1], in_=stats[:, s + 3:s + 4]),
            waits=W((sDV, m2c)), inc=sDV)
        a2[("nb", ti)] = nb_dv

    def plan_a2_h2(ti):
        par = ti % 2
        nb_dv = a2[("nb", ti)]
        hw = [(sDV, nb_dv)]
        if h2tile_war[par] is not None:
            hw.append(h2tile_war[par])
        h2_dv = p.add("vector", lambda e, ti=ti, par=par: e.tensor_scalar(
            out=h2tile[par], in0=xt[par], scalar1=m2r2[:, ti:ti + 1],
            scalar2=m2r2[:, NT + ti:NT + ti + 1],
            op0=OP.subtract, op1=OP.mult), waits=tuple(hw), inc=sDV)
        xt_free[par] = ("both", (sDV, h2_dv), (sST, x2_store_cnt[ti]))
        a2[("h2", ti)] = h2_dv

    def plan_a2_trT(ti):
        par = ti % 2
        h2_dv = a2[("h2", ti)]
        tpe = None
        for c in range(DC):
            tp = c % 4
            eng = "vector" if tp % 2 == 0 else "scalar"
            tw = [(sDV, h2_dv)] if c == 0 else []
            if tr_war[tp] is not None:
                tw.append((sDV if eng == "vector" else sSC, tr_war[tp]))
            if c < 4 and h2T_st[par] is not None:
                tw.append((sST, h2T_st[par]))
            tpe = p.add("tensor", lambda e, c=c, tp=tp, par=par: e.transpose(
                out=P_tr[tp], in_=h2tile[par][:, c * 128:(c + 1) * 128],
                identity=idb), waits=tuple(tw), inc=sPE)
            if eng == "vector":
                tr_war[tp] = p.add("vector", lambda e, c=c, par=par, tp=tp:
                                   e.tensor_copy(
                    out=h2T[par][:, c * 128:(c + 1) * 128], in_=P_tr[tp]),
                    waits=W((sPE, tpe)), inc=sDV)
            else:
                tr_war[tp] = p.add("scalar", lambda e, c=c, par=par, tp=tp:
                                   e.copy(
                    out=h2T[par][:, c * 128:(c + 1) * 128], in_=P_tr[tp]),
                    waits=W((sPE, tpe)), inc=sSC)
        h2tile_war[par] = (sPE, tpe)
        h2T_st[par] = p.add("sync", lambda e, ti=ti, par=par: e.dma_start(
            out=h2T_d[ti * 128:(ti + 1) * 128, :], in_=h2T[par]),
            waits=W((sDV, p.c(sDV)), (sSC, p.c(sSC))), inc=sST, n=16)
        h2t_store_cnt[ti] = h2T_st[par]

    def plan_q_chunks(kq, c0, first_wait=()):
        buf = kq % 2
        last = None
        w_cnts = {}
        for c in range(c0, c0 + 4):
            wcp = c % 4
            lw = list(first_wait) if c == c0 else []
            if q_wc_war[wcp] is not None:
                lw.append(q_wc_war[wcp])
            w_cnts[c] = p.add("scalar", lambda e, c=c, wcp=wcp, kq=kq:
                              e.dma_start(
                out=wc[wcp], in_=wT_d[c * 128:(c + 1) * 128,
                                      kq * 1024:(kq + 1) * 1024]),
                waits=tuple(lw), inc=sWT, n=16)
        for c in range(c0, c0 + 4):
            cp = c % 2
            wcp = c % 4
            aw = [(sWT, w_cnts[c0 + 3])]
            if q_a32_war[cp] is not None:
                aw.append(q_a32_war[cp])
            abs_c = p.add("scalar", lambda e, cp=cp, wcp=wcp: e.activation(
                out=a32[cp], in_=wc[wcp], func=AF.Abs), waits=tuple(aw),
                inc=sSC)
            sw = [q_sgn_war[cp]] if q_sgn_war[cp] else []
            abs_sc = p.add("scalar", lambda e, cp=cp, wcp=wcp: e.activation(
                out=sgn[cp], in_=wc[wcp], func=AF.Sign), waits=tuple(sw),
                inc=sSC)
            tw = [(sSC, abs_c)]
            if q_thr_war[cp] is not None:
                tw.append(q_thr_war[cp])
            t_pe = None
            for half in range(2):
                t_pe = p.add("tensor", lambda e, cp=cp, half=half: e.matmul(
                    P_thr[cp][:, half * 512:(half + 1) * 512], lhsT=honq,
                    rhs=a32[cp][:, half * 512:(half + 1) * 512],
                    start=True, stop=True),
                    waits=(tuple(tw) if half == 0 else ()),
                    inc=(sPE if half == 1 else None))
            mkw = [(sPE, t_pe)]
            if q_als_war[cp] is not None:
                mkw.append(q_als_war[cp])
            mk_dv = p.add("vector", lambda e, cp=cp: e.tensor_tensor(
                out=maskb[cp], in0=a32[cp], in1=P_thr[cp], op=OP.is_gt),
                waits=tuple(mkw), inc=sDV)
            q_a32_war[cp] = (sDV, mk_dv)
            ts_dv = p.add("vector", lambda e, cp=cp: e.tensor_tensor(
                out=m2q[cp], in0=sgn[cp], in1=P_thr[cp], op=OP.mult),
                inc=sDV)
            q_thr_war[cp] = (sDV, ts_dv)
            ww = [(sDV, ts_dv)]
            if wqh_war[buf] is not None and c == 0:
                ww.append(wqh_war[buf])
            wq_gp = p.add("gpsimd", lambda e, cp=cp, buf=buf, c=c:
                          e.tensor_tensor(
                out=wqh[buf][:, c * 1024:(c + 1) * 1024], in0=maskb[cp],
                in1=m2q[cp], op=OP.mult), waits=tuple(ww), inc=sGP)
            q_wc_war[wcp] = (sSC, abs_sc)
            q_sgn_war[cp] = (sDV, ts_dv)
            q_als_war[cp] = (sGP, wq_gp)
            last = wq_gp
        return last

    b_cnt = {}

    def plan_b_loads(kq, ti):
        par = ti % 2
        xp = ti % 3
        lw = [b_x2q_war[xp]] if b_x2q_war[xp] else []
        lw.append((sST, x2_store_cnt[ti]))
        b_cnt[("x", kq, ti)] = p.add(
            "sync", lambda e, ti=ti, xp=xp, kq=kq: e.dma_start(
                out=x2q[xp], in_=x2_d[ti * 128:(ti + 1) * 128,
                                      kq * 1024:(kq + 1) * 1024]),
            waits=tuple(lw), inc=sBL, n=16)
        lw2 = [b_h2t_war[par]] if b_h2t_war[par] else []
        lw2.append((sST, h2t_store_cnt[ti]))
        b_cnt[("h", kq, ti)] = p.add(
            "sync", lambda e, ti=ti, par=par: e.dma_start(
                out=h2Ts[par], in_=h2T_d[ti * 128:(ti + 1) * 128, :]),
            waits=tuple(lw2), inc=sBL, n=16)

    def plan_b_tile(kq, ti, wqh_ready):
        buf = kq % 2
        par = ti % 2
        xp = ti % 3
        x_c = b_cnt[("x", kq, ti)]
        h_c = b_cnt[("h", kq, ti)]
        mw = [(sBL, h_c), (sGP, wqh_ready)]
        if b_psum_war[par] is not None:
            mw.append(b_psum_war[par])
        f_pe = None
        for c in range(DC):
            for half in range(2):
                f_pe = p.add("tensor",
                             lambda e, c=c, half=half, par=par, buf=buf:
                             e.matmul(
                    P_ffn[par][:, half * 512:(half + 1) * 512],
                    lhsT=h2Ts[par][:, c * 128:(c + 1) * 128],
                    rhs=wqh[buf][:, c * 1024 + half * 512:
                                 c * 1024 + half * 512 + 512],
                    start=(c == 0), stop=(c == DC - 1)),
                    waits=(tuple(mw) if (c == 0 and half == 0) else ()),
                    inc=(sPE if (c == DC - 1 and half == 1) else None))
        b_h2t_war[par] = (sPE, f_pe)
        ev_dv = p.add("vector", lambda e, par=par, xp=xp:
                      e.scalar_tensor_tensor(
            out=x2q[xp], in0=P_ffn[par], scalar=2.0, in1=x2q[xp],
            op0=OP.mult, op1=OP.add), waits=W((sPE, f_pe), (sBL, x_c)),
            inc=sDV)
        b_psum_war[par] = (sDV, ev_dv)
        b_x2q_war[xp] = (sOUT, p.add(
            "sync", lambda e, ti=ti, xp=xp, kq=kq: e.dma_start(
                out=out_d[ti * 128:(ti + 1) * 128,
                          kq * 1024:(kq + 1) * 1024], in_=x2q[xp]),
            waits=W((sDV, ev_dv)), inc=sOUT, n=16))
        return f_pe

    plan_prolog()
    load_xtb(0)
    load_wqt_group(0)
    load_xtb(1)
    load_wqt_group(1)
    load_wqt_group(2)
    load_wqt_group(3)
    load_xtb(2)
    load_xtb(3)
    load_x("a1", 0)
    load_x("a1", 1)
    plan_wofold_h(0)
    plan_wofold_h(1)

    plan_a1_part1a(0)
    plan_a1_part1a(1)
    plan_a1_part1b(0)
    load_x("a1", 2)
    plan_a1_part1b(1)
    load_x("a1", 3)
    for ti in range(NT):
        plan_a1_part2(ti)
        if ti + 2 < NT:
            plan_a1_part1a(ti + 2)
            plan_a1_part1b(ti + 2)
            if ti + 4 < NT:
                load_x("a1", ti + 4)
        if ti + 4 < NT:
            load_xtb(ti + 4, extra=((sPE, a1[("qpe", ti)]),))
        if ti + 2 < NH:
            plan_wofold_h(ti + 2)

    if fold_state["pend"] is not None:
        plan_wofold_adds(*fold_state["pend"])
        fold_state["pend"] = None
    wof_store_cnt = p.c(sWF)
    del wof_q[:]
    for g in range(4):
        p.add("sync", lambda e, g=g: e.dma_start(
            out=wof.rearrange("p (h o) -> p h o", h=NH)[:, :, g * 1024:(g + 1) * 1024],
            in_=wof_d.rearrange("(h q) o -> q h o", q=128)[:, :, g * 1024:(g + 1) * 1024]),
            waits=(W((sWF, wof_store_cnt), (sPE, a1[("qpe", NT - 1)]))
                   if g == 0 else ()), inc=sWF, n=16)
        wof_q.append((sWF, p.c(sWF)))
    wof_ready[0] = wof_q[-1]

    plan_a2_loads(0)
    plan_a2_part1(0)
    plan_a2_loads(1)
    plan_a2_part1(1)
    for ti in range(NT):
        plan_a2_part2a(ti)
        plan_a2_h2(ti)
        if ti + 2 < NT:
            plan_a2_loads(ti + 2)
        plan_a2_trT(ti)
        if ti + 2 < NT:
            plan_a2_part1(ti + 2)

    a2_done = (sST, p.c(sST))
    wqh_ready = None
    for c0 in range(0, DC, 4):
        wqh_ready = plan_q_chunks(0, c0,
                                  first_wait=((a2_done,) if c0 == 0 else ()))
    for kq in range(NQ4):
        nxt = None
        f_pe = None
        plan_b_loads(kq, 0)
        for ti in range(NT):
            if ti + 1 < NT:
                plan_b_loads(kq, ti + 1)
            elif kq + 1 < NQ4:
                plan_b_loads(kq + 1, 0)
            f_pe = plan_b_tile(kq, ti, wqh_ready)
            if kq + 1 < NQ4:
                nxt = plan_q_chunks(kq + 1, ti * 4)
        wqh_war[kq % 2] = (sPE, f_pe)
        if kq + 1 < NQ4:
            wqh_ready = nxt
    out_final = p.c(sOUT)

    sems = {name: nc.alloc_semaphore(name=name) for name in p.counts}

    def expand(w):
        if w and w[0] == "both":
            return list(w[1:])
        return [w]

    def emit(eng, ops, final_wait=None):
        for fn, waits, inc, n in ops:
            for w in waits:
                for s, v in expand(w):
                    eng.wait_ge(sems[s], v)
            ins = fn(eng)
            if inc is not None:
                ins.then_inc(sems[inc], n)
        if final_wait is not None:
            eng.wait_ge(sems[final_wait[0]], final_wait[1])

    with nc.Block() as blk:
        blk.tensor(lambda e: emit(e, p.ops["tensor"]))
        blk.vector(lambda e: emit(e, p.ops["vector"]))
        blk.scalar(lambda e: emit(e, p.ops["scalar"]))
        blk.gpsimd(lambda e: emit(e, p.ops["gpsimd"]))
        blk.sync(lambda e: emit(e, p.ops["sync"],
                                final_wait=(sOUT, out_final)))
    return nc


_NC_CACHE = [None]


def kernel(L0_attn_hidden_states, L0_attn_Wq, L0_attn_bq, L0_attn_Wk,
           L0_attn_bk, L0_attn_Wv, L0_attn_bv, L0_attn_Wo, L0_attn_M,
           L0_attn_z, L0_ffn_weight):
    _ = (L0_attn_Wk, L0_attn_bk, L0_attn_Wv, L0_attn_bv)
    x = np.asarray(L0_attn_hidden_states, np.float32).reshape(8192, 4096)
    Wq = np.asarray(L0_attn_Wq, np.float32)
    bq = np.asarray(L0_attn_bq, np.float32)
    Wo = np.asarray(L0_attn_Wo, np.float32)
    M = np.asarray(L0_attn_M, np.float32)
    z = np.asarray(L0_attn_z, np.float32)
    w = np.asarray(L0_ffn_weight, np.float32)

    xT = np.ascontiguousarray(x.T)
    wqT = np.ascontiguousarray(
        Wq.reshape(32, 128, 4096)[::4].reshape(1024, 4096).T)
    bq_sel = np.ascontiguousarray(bq.reshape(32, 128)[::4].reshape(1, 1024))
    woT = np.ascontiguousarray(Wo.T)
    m_ = np.ascontiguousarray(M.reshape(1024, 128))
    zT = np.ascontiguousarray(z.T)
    wT = np.ascontiguousarray(w.T)
    idb = np.eye(128, dtype=ml_dtypes.bfloat16)
    idf = np.eye(128, dtype=np.float32)

    if _NC_CACHE[0] is None:
        _NC_CACHE[0] = build_program()
    nc = _NC_CACHE[0]
    in_maps = []
    for c in range(NCORES):
        rows = slice(c * S_CORE, (c + 1) * S_CORE)
        in_maps.append({
            "x_tm": np.ascontiguousarray(x[rows]),
            "xT": np.ascontiguousarray(xT[:, rows]),
            "wqT": wqT, "bq": bq_sel, "woT": woT, "m_": m_, "zT": zT,
            "wT": wT, "idb": idb, "idf": idf,
        })
    res = run_bass_kernel_spmd(nc, in_maps, core_ids=list(range(NCORES)))
    out = np.concatenate([r["out"] for r in res.results], axis=0)
    return out.reshape(1, 8192, 4096)
